# version 13
# baseline (speedup 1.0000x reference)
import numpy as np
from contextlib import ExitStack
import concourse.bass as bass
import concourse.mybir as mybir
from concourse.bass_utils import run_bass_kernel_spmd
from concourse.alu_op_type import AluOpType as OP

F32 = mybir.dt.float32
BF16 = mybir.dt.bfloat16
U32 = mybir.dt.uint32
AF = mybir.ActivationFunctionType

S = 4096
D = 1024
NCH = 32
DBGP = 0
DOTMODE = "actaccum"
NSC = 8
NEXP = 16384
INTER = 4096

DEBUG = 0
N_CORES = 8
PHASES = ("p1a", "p1b", "p2", "peer", "p3")


class Rec:
    def __init__(self):
        self.ops = []
        self.cnt = {}
        self.dma_sems = set()

    def c(self, s):
        return self.cnt.get(s, 0)

    scope = "init"

    def emit(self, eng, fn, waits=(), inc=None, inc_n=1, dma=False):
        item = dict(eng=eng, fn=fn, waits=[(s, v) for (s, v) in waits if v > 0],
                    inc=None, scope=self.scope)
        if inc is not None:
            self.cnt[inc] = self.c(inc) + inc_n
            item["inc"] = (inc, inc_n)
            if dma:
                self.dma_sems.add(inc)
        self.ops.append(item)

    def all_dma_waits(self):
        return [(s, self.c(s)) for s in sorted(self.dma_sems)]


def vdone(e):
    e.drain()
    return e.nop()


def build(n_dbg=0):
    nc = bass.Bass("TRN2", target_bir_lowering=False)

    x_d = nc.dram_tensor("x", [S, D], F32, kind="ExternalInput").ap()
    wq_d = nc.dram_tensor("wq", [D, D], F32, kind="ExternalInput").ap()
    wk_d = nc.dram_tensor("wk", [D, D], F32, kind="ExternalInput").ap()
    wv_d = nc.dram_tensor("wv", [D, D], F32, kind="ExternalInput").ap()
    wg_d = nc.dram_tensor("wg", [D, D], F32, kind="ExternalInput").ap()
    wo_d = nc.dram_tensor("wo", [D, D], F32, kind="ExternalInput").ap()
    bias5_d = nc.dram_tensor("bias5", [5, D], F32, kind="ExternalInput").ap()
    wqf_d = nc.dram_tensor("wqf", [512, D], F32, kind="ExternalInput").ap()
    keys_d = nc.dram_tensor("keys", [4, 128, 2, 64], F32, kind="ExternalInput").ap()
    down_d = nc.dram_tensor("down", [NEXP, D], F32, kind="ExternalInput").ap()
    up_d = nc.dram_tensor("up", [NEXP, D], F32, kind="ExternalInput").ap()
    wgf_d = nc.dram_tensor("wgf", [INTER, D], F32, kind="ExternalInput").ap()
    wuf_d = nc.dram_tensor("wuf", [INTER, D], F32, kind="ExternalInput").ap()
    wdf_d = nc.dram_tensor("wdf", [D, INTER], F32, kind="ExternalInput").ap()
    eye_d = nc.dram_tensor("eye", [128, 128], F32, kind="ExternalInput").ap()

    out_d = nc.dram_tensor("out", [S, D], F32, kind="ExternalOutput").ap()
    dbg_d = None
    if n_dbg:
        dbg_d = nc.dram_tensor("dbg", [n_dbg, S, D], BF16, kind="ExternalOutput").ap()
    dbgP = None
    if DBGP:
        dbgP = nc.dram_tensor("dbgP", [6, 128, 64], F32, kind="ExternalOutput").ap()

    tbl2 = nc.dram_tensor("tbl2", [NEXP, 2 * D], BF16).ap()
    wgbf = nc.dram_tensor("wgbf", [INTER, D], BF16).ap()
    wubf = nc.dram_tensor("wubf", [INTER, D], BF16).ap()
    wdbf = nc.dram_tensor("wdbf", [D, INTER], BF16).ap()
    wqfbf = nc.dram_tensor("wqfbf", [512, D], BF16).ap()
    bias_bf = nc.dram_tensor("bias_bf", [5, D], BF16).ap()
    xn_d = nc.dram_tensor("xn_td", [S, D], BF16).ap()
    s_d = nc.dram_tensor("s_td", [S, D], BF16).ap()
    exp_d = nc.dram_tensor("exp_td", [S, D], BF16).ap()
    mlp1_d = nc.dram_tensor("mlp1_td", [S, D], BF16).ap()

    rec = Rec()
    E = rec.emit
    C = rec.c

    with ExitStack() as ctx:
        en = ctx.enter_context

        eyebf = en(nc.sbuf_tensor("eyebf", [128, 128], BF16))
        onesr = en(nc.sbuf_tensor("onesr", [1, 512], BF16))
        zrow = en(nc.sbuf_tensor("zrow", [1, 512], BF16))
        iota64 = en(nc.sbuf_tensor("iota64", [128, 64], F32))
        epsv = en(nc.sbuf_tensor("epsv", [128, 1], F32))
        brow = en(nc.sbuf_tensor("brow", [1, 5 * D], BF16))
        kvsb = en(nc.sbuf_tensor("kvsb", [128, 512], BF16))
        wqfT = en(nc.sbuf_tensor("wqfT", [128, 8 * 512], BF16))
        keysT = en(nc.sbuf_tensor("keysT", [64, 8 * 128], BF16))
        pall = en(nc.psum_tensor("pall", [128, 4096], F32))

        def bank(b, n=1):
            return pall[:, 512 * b:512 * (b + n)]

        for i in range(2):
            E("gpsimd", lambda e, i=i: e.dma_start(
                out=wgbf[i * 2048:(i + 1) * 2048, :],
                in_=wgf_d[i * 2048:(i + 1) * 2048, :]), inc="cast", inc_n=16, dma=True)
            E("gpsimd", lambda e, i=i: e.dma_start(
                out=wubf[i * 2048:(i + 1) * 2048, :],
                in_=wuf_d[i * 2048:(i + 1) * 2048, :]), inc="cast", inc_n=16, dma=True)
            E("gpsimd", lambda e, i=i: e.dma_start(
                out=wdbf[i * 512:(i + 1) * 512, :],
                in_=wdf_d[i * 512:(i + 1) * 512, :]), inc="cast", inc_n=16, dma=True)
        E("gpsimd", lambda e: e.dma_start(out=wqfbf, in_=wqf_d),
          inc="cast", inc_n=16, dma=True)
        cast_done = C("cast")
        for i in range(4):
            E("gpsimd", lambda e, i=i: e.dma_start(
                out=tbl2[i * 4096:(i + 1) * 4096, 0:D],
                in_=down_d[i * 4096:(i + 1) * 4096, :]), inc="castT", inc_n=16, dma=True)
        for i in range(4):
            E("gpsimd", lambda e, i=i: e.dma_start(
                out=tbl2[i * 4096:(i + 1) * 4096, D:2 * D],
                in_=up_d[i * 4096:(i + 1) * 4096, :]), inc="castT", inc_n=16, dma=True)
        cast_doneT = C("castT")

        E("gpsimd", lambda e: e.iota(iota64[:], [[1, 64]], channel_multiplier=0,
                                     allow_small_or_imprecise_dtypes=True), inc="pool")
        iota_done = C("pool")

        E("vector", lambda e: e.memset(onesr[:], 1.0))
        E("vector", lambda e: e.memset(zrow[:], 0.0))
        E("vector", lambda e: e.memset(epsv[:], 1e-5))
        E("vector", vdone, inc="dve")
        consts_done = C("dve")

        with ExitStack() as c1:
            wqs = c1.enter_context(nc.sbuf_tensor("wqs", [128, 8192], BF16))
            wgs = c1.enter_context(nc.sbuf_tensor("wgs", [128, 8192], BF16))
            wos = c1.enter_context(nc.sbuf_tensor("wos", [128, 8192], BF16))
            xnf = [c1.enter_context(nc.sbuf_tensor(f"xnf{i}", [128, 4096], BF16))
                   for i in range(2)]
            esb = c1.enter_context(nc.sbuf_tensor("esb", [128, 512], BF16))
            rsb = c1.enter_context(nc.sbuf_tensor("rsb", [128, 512], BF16))
            qps = c1.enter_context(nc.sbuf_tensor("qps", [128, 4096], BF16))
            gts = c1.enter_context(nc.sbuf_tensor("gts", [128, 4096], BF16))
            gas = c1.enter_context(nc.sbuf_tensor("gas", [128, 4096], BF16))
            ssb = [c1.enter_context(nc.sbuf_tensor(f"ssb{i}", [128, 1024], BF16))
                   for i in range(2)]
            s1 = ExitStack()
            wks = s1.enter_context(nc.sbuf_tensor("wks", [128, 8192], BF16))
            wvs = s1.enter_context(nc.sbuf_tensor("wvs", [128, 8192], BF16))
            kpsb = s1.enter_context(nc.sbuf_tensor("kpsb", [128, 1024], BF16))
            vpsb = s1.enter_context(nc.sbuf_tensor("vpsb", [128, 1024], BF16))
            s2 = ExitStack()
            wstg = [s2.enter_context(nc.sbuf_tensor(f"wstg{i}", [128, 1024], F32))
                    for i in range(2)]
            xt = [s2.enter_context(nc.sbuf_tensor(f"xt{i}", [128, 1024], F32))
                  for i in range(2)]
            xnt1 = [s2.enter_context(nc.sbuf_tensor(f"xnt1_{i}", [128, 1024], BF16))
                    for i in range(2)]
            stats = s2.enter_context(nc.sbuf_tensor("stats", [128, 12], F32))
            mv = [s2.enter_context(nc.sbuf_tensor(f"mv{i}", [128, 2], F32)) for i in range(2)]
            rst = [s2.enter_context(nc.sbuf_tensor(f"rst{i}", [128, 2], F32)) for i in range(2)]
            kstage = s2.enter_context(nc.sbuf_tensor("kstage", [128, 64], F32))
            eye = s2.enter_context(nc.sbuf_tensor("eyef", [128, 128], F32))

            E("sync", lambda e: e.dma_start(out=eye[:], in_=eye_d),
              inc="ldc", inc_n=16, dma=True)
            ld_eye = C("ldc")
            E("vector", lambda e: e.tensor_copy(out=eyebf[:], in_=eye[:]),
              waits=[("ldc", ld_eye)])
            E("vector", vdone, inc="dve")
            eyebf_done = C("dve")
            E("gpsimd", lambda e: e.dma_start(out=bias_bf, in_=bias5_d),
              inc="ldb", inc_n=16, dma=True)
            E("sync", lambda e: e.dma_start(
                out=brow[:], in_=bias_bf.rearrange("a b -> (a b)").unsqueeze(0)),
                waits=[("ldb", C("ldb"))], inc="ldb2", inc_n=16, dma=True)
            E("gpsimd", lambda e: e.nop(), waits=[("ldb2", C("ldb2"))], inc="pool")

            wconv = []
            for wsb, wd in ((wks, wk_d), (wvs, wv_d), (wqs, wq_d), (wgs, wg_d), (wos, wo_d)):
                for ic in range(8):
                    wconv.append((wsb, wd, ic))
            wstg_free = {}

            def emit_wconv(j):
                wsb, wd, ic = wconv[j]
                stg = wstg[j % 2]
                sem = f"ldw{j % 2}"
                wl = [("pool", wstg_free[j - 2])] if j - 2 in wstg_free else []
                E("sync", lambda e, stg=stg, wd=wd, ic=ic: e.dma_start(
                    out=stg[:], in_=wd[ic * 128:(ic + 1) * 128, :]), waits=wl,
                    inc=sem, inc_n=16, dma=True)
                xv = C(sem)
                E("gpsimd", lambda e, wsb=wsb, stg=stg, ic=ic: e.tensor_copy(
                    out=wsb[:, ic * 1024:(ic + 1) * 1024], in_=stg[:]),
                    waits=[(sem, xv)], inc="pool")
                wstg_free[j] = C("pool")

            def emit_keys():
                for ph in range(8):
                    p_, h_ = ph // 4, ph % 4
                    E("sync", lambda e, h_=h_, p_=p_: e.dma_start(
                        out=kstage[:, 0:64], in_=keys_d[h_, :, p_, :]),
                        waits=[("act", C("act"))], inc="ldk", inc_n=16, dma=True)
                    kl = C("ldk")
                    E("tensor", lambda e: e.transpose(
                        out=pall[0:64, 0:128], in_=kstage[:], identity=eye[:]),
                        waits=[("ldk", kl), ("ldc", ld_eye), ("act", C("act"))], inc="pe")
                    kp = C("pe")
                    E("scalar", lambda e, ph=ph: e.activation(
                        out=keysT[0:64, ph * 128:(ph + 1) * 128],
                        in_=pall[0:64, 0:128], func=AF.Copy), waits=[("pe", kp)],
                        inc="act")

            def emit_wqft():
                for ic in range(8):
                    E("sync", lambda e, ic=ic: e.dma_start_transpose(
                        out=wqfT[:, ic * 512:(ic + 1) * 512],
                        in_=wqfbf[:, ic * 128:(ic + 1) * 128]),
                        waits=([("cast", cast_done)] if ic == 0 else []),
                        inc="wqf", inc_n=16, dma=True)

            rec.scope = "p1a"
            p1a_st = {}
            p1a_x_act = {}
            if "p1a" in PHASES:
                for ci in range(NCH):
                    b = ci % 2
                    t0 = ci * 128
                    sx = f"ldx{b}"
                    wl = [("act", p1a_x_act[ci - 2])] if ci - 2 in p1a_x_act else []
                    E("sync", lambda e, t0=t0, b=b: e.dma_start(
                        out=xt[b][:], in_=x_d[t0:t0 + 128, :]), waits=wl,
                        inc=sx, inc_n=16, dma=True)
                    xld = C(sx)
                    E("vector", lambda e, b=b: e.bn_stats(out=stats[:, 0:6],
                                                          in_=xt[b][:, 0:512]),
                      waits=[(sx, xld), ("dve", consts_done)])
                    E("vector", lambda e, b=b: e.bn_stats(out=stats[:, 6:12],
                                                          in_=xt[b][:, 512:1024]))
                    E("vector", lambda e: e.drain())
                    E("vector", lambda e, b=b: e.bn_aggr(out=mv[b][:], in_=stats[:]))
                    E("vector", vdone, inc="dve")
                    E("scalar", lambda e, b=b: e.activation(
                        out=rst[b][:, 0:1], in_=mv[b][:, 1:2], func=AF.Sqrt, bias=epsv[:]),
                        waits=[("dve", C("dve"))], inc="act")
                    E("vector", lambda e, b=b: e.reciprocal(out=rst[b][:, 1:2],
                                                            in_=rst[b][:, 0:1]),
                      waits=[("act", C("act"))])
                    E("vector", lambda e: e.drain())
                    E("vector", lambda e, b=b: e.tensor_scalar(
                        out=rst[b][:, 0:1], in0=mv[b][:, 0:1], scalar1=rst[b][:, 1:2],
                        scalar2=-1.0, op0=OP.mult, op1=OP.mult))
                    E("vector", vdone, inc="dve")
                    ssem = f"sxn{b}"
                    wl2 = [("dve", C("dve"))]
                    if ci - 2 in p1a_st:
                        wl2.append(p1a_st[ci - 2])
                    E("scalar", lambda e, b=b: e.activation(
                        out=xnt1[b][:], in_=xt[b][:], func=AF.Identity,
                        scale=rst[b][:, 1:2], bias=rst[b][:, 0:1]), waits=wl2, inc="act")
                    p1a_x_act[ci] = C("act")
                    E("scalar", lambda e, t0=t0, b=b: e.dma_start(
                        out=xn_d[t0:t0 + 128, :], in_=xnt1[b][:]),
                        waits=[("act", p1a_x_act[ci])], inc=ssem, inc_n=16, dma=True)
                    p1a_st[ci] = (ssem, C(ssem))
                    if ci < 20:
                        emit_wconv(2 * ci)
                        emit_wconv(2 * ci + 1)
                    if ci == 20:
                        emit_keys()
                    if ci == 21:
                        emit_wqft()
            w_done = C("pool")
            wkv_done = wstg_free.get(15, w_done)
            s2.close()

            def xn_store_waits(sc):
                got = {}
                for ci in range(min(sc * 4 + 4, NCH)):
                    if ci in p1a_st:
                        sname, v = p1a_st[ci]
                        got[sname] = max(got.get(sname, 0), v)
                return list(got.items())

            if "p1b" in PHASES:
                rec.scope = "p1b"
                k_evac_act = {0: 0}
                v_evac_act = {0: 0}
                E("tensor", lambda e: e.matmul(
                    out=bank(6), lhsT=onesr[:, 0:128], rhs=zrow[:],
                    start=True, stop=False, skip_group_check=True),
                    waits=[("dve", consts_done)])
                p1b_xnf_free = {}
                for sc in range(NSC):
                    b = sc % 2
                    t0 = sc * 512
                    xsem = f"xnf{b}"
                    wl = xn_store_waits(sc)
                    if sc - 2 in p1b_xnf_free:
                        wl.append(("pe", p1b_xnf_free[sc - 2]))
                    for ic in range(8):
                        E("sync", lambda e, t0=t0, ic=ic, b=b: e.dma_start_transpose(
                            out=xnf[b][:, ic * 512:(ic + 1) * 512],
                            in_=xn_d[t0:t0 + 512, ic * 128:(ic + 1) * 128]),
                            waits=(wl if ic == 0 else []), inc=xsem, inc_n=16, dma=True)
                    xnf_ld = C(xsem)
                    for cl in range(4):
                        ci = sc * 4 + cl
                        for oc in range(2):
                            for ic in range(8):
                                w0 = []
                                if oc == 0 and ic == 0:
                                    w0 = [(xsem, xnf_ld), ("act", k_evac_act[0]),
                                          ("dve", C("dve"))]
                                    if ci == 0:
                                        w0.append(("pool", wkv_done))
                                E("tensor", lambda e, b=b, cl=cl, ic=ic, oc=oc: e.matmul(
                                    out=bank(oc),
                                    lhsT=xnf[b][:, ic * 512 + cl * 128:ic * 512 + cl * 128 + 128],
                                    rhs=wks[:, ic * 1024 + oc * 512:ic * 1024 + (oc + 1) * 512],
                                    start=(ic == 0), stop=False), waits=w0)
                            E("tensor", lambda e, oc=oc: e.matmul(
                                out=bank(oc), lhsT=onesr[:, 0:128],
                                rhs=brow[:, D + oc * 512:D + (oc + 1) * 512],
                                start=False, stop=True), inc=("pe" if oc == 1 else None))
                        kpe = C("pe")
                        for pc in range(2):
                            E("scalar", lambda e, pc=pc: e.activation(
                                out=esb[:], in_=pall[:, pc * 512:(pc + 1) * 512],
                                func=AF.Exp),
                                waits=[("dve", C("dve"))] + ([("pe", kpe)]
                                                             if pc == 0 else []))
                            E("scalar", lambda e, pc=pc: e.activation(
                                out=rsb[:], in_=pall[:, pc * 512:(pc + 1) * 512],
                                func=AF.Relu), inc="act")
                            kact = C("act")
                            E("vector", lambda e, pc=pc: e.scalar_tensor_tensor(
                                out=kpsb[:, pc * 512:(pc + 1) * 512], in0=esb[:],
                                scalar=1.0,
                                in1=rsb[:], op0=OP.min, op1=OP.add),
                                waits=[("act", kact)])
                            E("vector", vdone, inc="dve")
                        k_evac_act[0] = kact
                        kdve = C("dve")
                        for oc in range(2):
                            for ic in range(8):
                                w0 = [("act", v_evac_act[0])] if (oc == 0 and ic == 0) else []
                                E("tensor", lambda e, b=b, cl=cl, ic=ic, oc=oc: e.matmul(
                                    out=bank(2 + oc),
                                    lhsT=xnf[b][:, ic * 512 + cl * 128:ic * 512 + cl * 128 + 128],
                                    rhs=wvs[:, ic * 1024 + oc * 512:ic * 1024 + (oc + 1) * 512],
                                    start=(ic == 0), stop=False), waits=w0)
                            E("tensor", lambda e, oc=oc: e.matmul(
                                out=bank(2 + oc), lhsT=onesr[:, 0:128],
                                rhs=brow[:, 2 * D + oc * 512:2 * D + (oc + 1) * 512],
                                start=False, stop=True), inc=("pe" if oc == 1 else None))
                        vpe = C("pe")
                        E("scalar", lambda e: e.activation(out=vpsb[:], in_=bank(2, 2),
                                                           func=AF.Copy),
                          waits=[("pe", vpe), ("dve", C("dve"))], inc="act")
                        vact = C("act")
                        v_evac_act[0] = vact
                        for h in range(16):
                            par, col = h % 2, h // 2
                            E("tensor", lambda e, h=h, ci=ci, par=par, col=col: e.matmul(
                                out=pall[par * 64:par * 64 + 64,
                                         6 * 512 + col * 64:6 * 512 + (col + 1) * 64],
                                lhsT=kpsb[:, h * 64:(h + 1) * 64],
                                rhs=vpsb[:, h * 64:(h + 1) * 64],
                                start=False, stop=(ci == NCH - 1),
                                skip_group_check=True),
                                waits=([("dve", kdve), ("act", vact)] if h == 0 else []),
                                inc=("pe" if h == 15 else None))
                    p1b_xnf_free[sc] = C("pe")
                kv_done = C("pe")
                E("vector", lambda e: e.tensor_copy(out=kvsb[:], in_=bank(6)),
                  waits=[("pe", kv_done)])
                E("vector", vdone, inc="dve")
            kv_evac = C("dve")
            s1.close()


            p2_ss_free = {}

            def p2_piece_units(sc, piece, st):
                b = sc % 2
                t0 = sc * 512
                units = []
                if piece == 0:
                    def mk_q_mm(wv, oc4):
                        def u():
                            oc = wv * 4 + oc4
                            for ic in range(8):
                                w0 = []
                                if oc4 == 0 and ic == 0:
                                    if wv == 0:
                                        w0 = [(st["xsem"], st["xnf_ld"]),
                                              ("act", C("act")), ("dve", C("dve"))]
                                    else:
                                        w0 = [("act", st["q_relu0"])]
                                E("tensor", lambda e, b=b, ic=ic, oc=oc, oc4=oc4: e.matmul(
                                    out=bank(oc4),
                                    lhsT=wqs[:, ic * 1024 + oc * 128:ic * 1024 + oc * 128 + 128],
                                    rhs=xnf[b][:, ic * 512:(ic + 1) * 512],
                                    start=(ic == 0), stop=False), waits=w0)
                            E("tensor", lambda e, oc=oc, oc4=oc4: e.matmul(
                                out=bank(oc4), lhsT=brow[:, oc * 128:(oc + 1) * 128],
                                rhs=onesr[:, 0:512], start=False, stop=True),
                                inc=("pe" if oc4 == 3 else None))
                        return u

                    def mk_q_elu(wv):
                        def u():
                            qpe = C("pe")
                            for pc in range(4):
                                E("scalar", lambda e, pc=pc: e.activation(
                                    out=esb[:], in_=pall[:, pc * 512:(pc + 1) * 512],
                                    func=AF.Exp),
                                    waits=[("dve", C("dve"))] + ([("pe", qpe)]
                                                                 if pc == 0 else []))
                                E("scalar", lambda e, pc=pc: e.activation(
                                    out=rsb[:], in_=pall[:, pc * 512:(pc + 1) * 512],
                                    func=AF.Relu), inc="act")
                                r_act = C("act")
                                E("vector", lambda e, wv=wv, pc=pc: e.scalar_tensor_tensor(
                                    out=qps[:, wv * 2048 + pc * 512:wv * 2048 + (pc + 1) * 512],
                                    in0=esb[:], scalar=1.0,
                                    in1=rsb[:], op0=OP.min, op1=OP.add),
                                    waits=[("act", r_act)])
                                E("vector", vdone, inc="dve")
                                st["q_relu0"] = r_act
                            st["q_dve"] = C("dve")
                        return u

                    for wv in range(2):
                        for oc4 in range(4):
                            units.append(mk_q_mm(wv, oc4))
                        units.append(mk_q_elu(wv))
                elif piece == 1:
                    def mk_g_mm(wv, oc4):
                        def u():
                            oc = wv * 4 + oc4
                            for ic in range(8):
                                w0 = []
                                if oc4 == 0 and ic == 0:
                                    w0 = ([("dve", st["q_dve"])] if wv == 0
                                          else [("act", st["g_sig0"])])
                                E("tensor", lambda e, b=b, ic=ic, oc=oc, oc4=oc4: e.matmul(
                                    out=bank(oc4),
                                    lhsT=wgs[:, ic * 1024 + oc * 128:ic * 1024 + oc * 128 + 128],
                                    rhs=xnf[b][:, ic * 512:(ic + 1) * 512],
                                    start=(ic == 0), stop=False), waits=w0)
                            E("tensor", lambda e, oc=oc, oc4=oc4: e.matmul(
                                out=bank(oc4),
                                lhsT=brow[:, 3 * D + oc * 128:3 * D + oc * 128 + 128],
                                rhs=onesr[:, 0:512], start=False, stop=True),
                                inc=("pe" if oc4 == 3 else None))
                        return u

                    def mk_g_sig(wv):
                        def u():
                            gpe = C("pe")
                            E("scalar", lambda e, wv=wv: e.activation(
                                out=gts[:, wv * 2048:(wv + 1) * 2048], in_=pall[:, 0:2048],
                                func=AF.Sigmoid), waits=[("pe", gpe)], inc="act")
                            st["g_sig0"] = C("act")
                            if wv == 1:
                                st["g_sig1"] = C("act")
                        return u

                    for wv in range(2):
                        for oc4 in range(4):
                            units.append(mk_g_mm(wv, oc4))
                        units.append(mk_g_sig(wv))
                elif piece == 2:
                    def mk_attn(wv, hh4):
                        def u():
                            hh = wv * 4 + hh4
                            for par in range(2):
                                h = 2 * hh + par
                                w0 = []
                                if hh4 == 0 and par == 0:
                                    w0 = ([("act", st["g_sig1"])] if wv == 0
                                          else [("dve", C("dve"))])
                                E("tensor", lambda e, hh4=hh4, par=par, h=h: e.matmul(
                                    out=pall[par * 64:par * 64 + 64,
                                             hh4 * 512:(hh4 + 1) * 512],
                                    lhsT=kvsb[par * 64:par * 64 + 64,
                                              (h // 2) * 64:(h // 2 + 1) * 64],
                                    rhs=qps[par * 64:par * 64 + 64,
                                            (h // 2) * 512:(h // 2 + 1) * 512],
                                    start=True, stop=(par == 1), skip_group_check=True),
                                    waits=w0, inc=("pe" if par == 1 else None))
                            ape = C("pe")
                            E("vector", lambda e, hh4=hh4, hh=hh: e.tensor_tensor(
                                out=gas[:, hh * 512:(hh + 1) * 512], in0=bank(hh4),
                                in1=gts[:, hh * 512:(hh + 1) * 512], op=OP.mult),
                                waits=[("pe", ape)])
                            if hh4 == 3:
                                E("vector", vdone, inc="dve")
                                if wv == 1:
                                    st["ga_dve"] = C("dve")
                        return u

                    for wv in range(2):
                        for hh4 in range(4):
                            units.append(mk_attn(wv, hh4))
                elif piece == 3:
                    def mk_wo(tsub, oc):
                        def u():
                            sb = ssb[tsub % 2]
                            ssem = f"ss{tsub % 2}"
                            for ic in range(8):
                                w0 = []
                                if oc == 0 and ic == 0:
                                    w0 = [("dve", st["ga_dve"]), ("act", C("act"))]
                                E("tensor", lambda e, ic=ic, oc=oc, tsub=tsub: e.matmul(
                                    out=bank(oc),
                                    lhsT=gas[:, ic * 512 + tsub * 128:ic * 512 + tsub * 128 + 128],
                                    rhs=wos[:, ic * 1024 + oc * 512:ic * 1024 + (oc + 1) * 512],
                                    start=(ic == 0), stop=False), waits=w0)
                            E("tensor", lambda e, oc=oc: e.matmul(
                                out=bank(oc), lhsT=onesr[:, 0:128],
                                rhs=brow[:, 4 * D + oc * 512:4 * D + (oc + 1) * 512],
                                start=False, stop=True), inc=("pe" if oc == 1 else None))
                            if oc == 1:
                                spe = C("pe")
                                wse = [("pe", spe)]
                                if ssem in p2_ss_free:
                                    wse.append((ssem, p2_ss_free[ssem]))
                                E("scalar", lambda e, sb=sb: e.activation(
                                    out=sb[:, 0:512], in_=bank(0), func=AF.Copy),
                                    waits=wse)
                                E("scalar", lambda e, sb=sb: e.activation(
                                    out=sb[:, 512:1024], in_=bank(1), func=AF.Copy),
                                    inc="act")
                                E("scalar", lambda e, t0=t0, tsub=tsub, sb=sb: e.dma_start(
                                    out=s_d[t0 + tsub * 128:t0 + tsub * 128 + 128, :],
                                    in_=sb[:]),
                                    waits=[("act", C("act"))], inc=ssem, inc_n=16,
                                    dma=True)
                                p2_ss_free[ssem] = C(ssem)
                        return u

                    for tsub in range(4):
                        for oc in range(2):
                            units.append(mk_wo(tsub, oc))
                return units

            if "peer" in PHASES:
                rec.scope = "peer"
                with ExitStack() as c2:
                    xnt2 = [c2.enter_context(nc.sbuf_tensor(f"xnt2_{i}", [128, 4096], BF16))
                            for i in range(2)]
                    qfs = [c2.enter_context(nc.sbuf_tensor(f"qfs{i}", [64, 8 * 512], BF16))
                           for i in range(2)]
                    NT = 12
                    gt = [c2.enter_context(nc.sbuf_tensor(f"gt{i}", [128, 2048], BF16))
                          for i in range(NT)]
                    sxy = c2.enter_context(nc.sbuf_tensor("sxy", [128, 64], F32))
                    ixy = c2.enter_context(nc.sbuf_tensor("ixy", [128, 64], U32))
                    asf = c2.enter_context(nc.sbuf_tensor("asf", [128, 256], F32))
                    aif = c2.enter_context(nc.sbuf_tensor("aif", [128, 256], F32))
                    ixyf = asf[:, 0:64]
                    sc8 = c2.enter_context(nc.sbuf_tensor("sc8", [128, 32], F32))
                    posu = c2.enter_context(nc.sbuf_tensor("posu", [128, 32], U32))
                    posf = c2.enter_context(nc.sbuf_tensor("posf", [128, 32], F32))
                    fidxf = c2.enter_context(nc.sbuf_tensor("fidxf", [128, 32], F32))
                    fidxu2 = c2.enter_context(nc.sbuf_tensor("fidxu2", [128, 64], U32))
                    negm4 = c2.enter_context(nc.sbuf_tensor("negm4", [128, 4], F32))
                    esc2 = c2.enter_context(nc.sbuf_tensor("esc2", [128, 64], F32))
                    sw32 = c2.enter_context(nc.sbuf_tensor("sw32", [128, 32], F32))
                    ssum42 = c2.enter_context(nc.sbuf_tensor("ssum42", [128, 16], F32))
                    wv32 = c2.enter_context(nc.sbuf_tensor("wv32", [128, 32], F32))
                    ccs = c2.enter_context(nc.sbuf_tensor("ccs", [128, 32], F32))
                    diag8 = c2.enter_context(nc.sbuf_tensor("diag8_0", [128, 8 * 128], BF16))
                    junkb = c2.enter_context(nc.sbuf_tensor("junkb", [128, 1024], BF16))
                    expt = c2.enter_context(nc.sbuf_tensor("expt", [128, 1024], BF16))

                    gt_free = {}
                    exp_free = 0
                    first_vec = [True]
                    fidx_dve_c = {}
                    mmc = {}

                    def qf_wave_units(sc, wv, st):
                        b = sc % 2
                        units = []

                        def mk_mm(ph4):
                            def u():
                                ph = wv * 4 + ph4
                                ph_off = (ph // 4) * 256 + (ph % 4) * 64
                                for ic in range(8):
                                    w0 = []
                                    if ph4 == 0 and ic == 0:
                                        w0 = [(st["xsem"], st["xnf_ld"]),
                                              ("act", C("act")), ("dve", C("dve"))]
                                    E("tensor", lambda e, b=b, ic=ic, ph=ph, ph4=ph4,
                                        ph_off=ph_off: e.matmul(
                                        out=pall[0:64, (4 + ph4) * 512:(4 + ph4 + 1) * 512],
                                        lhsT=wqfT[:, ic * 512 + ph_off:ic * 512 + ph_off + 64],
                                        rhs=xnf[b][:, ic * 512:(ic + 1) * 512],
                                        start=(ic == 0), stop=(ic == 7),
                                        skip_group_check=True), waits=w0,
                                        inc=("pe" if (ic == 7) else None))
                            return u

                        for ph4 in range(4):
                            units.append(mk_mm(ph4))

                        def evac():
                            qf_pe = C("pe")
                            for ph4 in range(4):
                                ph = wv * 4 + ph4
                                E("scalar", lambda e, ph=ph, ph4=ph4, b=b: e.activation(
                                    out=qfs[b][0:64, ph * 512:(ph + 1) * 512],
                                    in_=pall[0:64, (4 + ph4) * 512:(4 + ph4 + 1) * 512],
                                    func=AF.Copy),
                                    waits=([("pe", qf_pe)] if ph4 == 0 else []),
                                    inc="act")
                        units.append(evac)
                        return units

                    def emit_qf_wave(sc, wv, st):
                        for u in qf_wave_units(sc, wv, st):
                            u()

                    def emit_schead_a(sc):
                        b = sc % 2
                        t0 = sc * 512
                        xsem = f"xnf{b}"
                        tsem = f"xnt{b}"
                        for ic in range(8):
                            E("sync", lambda e, t0=t0, ic=ic, b=b: e.dma_start_transpose(
                                out=xnf[b][:, ic * 512:(ic + 1) * 512],
                                in_=xn_d[t0:t0 + 512, ic * 128:(ic + 1) * 128]),
                                waits=([("pe", C("pe")), ("dve", C("dve"))]
                                       if ic == 0 else []),
                                inc=xsem, inc_n=16, dma=True)
                        xnf_ld = C(xsem)
                        E("sync", lambda e, t0=t0, b=b: e.dma_start(
                            out=xnt2[b][:].rearrange("p (c d) -> p c d", d=1024),
                            in_=xn_d[t0:t0 + 512, :].rearrange("(c p) d -> p c d", p=128)),
                            waits=[("dve", C("dve"))], inc=tsem, inc_n=16, dma=True)
                        xnt_ld = C(tsem)
                        return dict(xsem=xsem, xnf_ld=xnf_ld, tsem=tsem, xnt_ld=xnt_ld)

                    def emit_sim(ci, st):
                        sc, cl = ci // 4, ci % 4
                        b = sc % 2
                        pb0 = (4 + 2 * (ci % 2)) * 512
                        for ph in range(8):
                            w0 = []
                            if ph == 0:
                                w0 = [("act", C("act")), ("dve", C("dve"))]
                            E("tensor", lambda e, ph=ph, cl=cl, b=b, pb0=pb0: e.matmul(
                                out=pall[:, pb0 + ph * 128:pb0 + (ph + 1) * 128],
                                lhsT=qfs[b][0:64, ph * 512 + cl * 128:ph * 512 + cl * 128 + 128],
                                rhs=keysT[0:64, ph * 128:(ph + 1) * 128],
                                start=True, stop=True, skip_group_check=True), waits=w0,
                                inc=("pe" if ph == 7 else None))
                        return C("pe")

                    def emit_topk(ci, st, sim_pe):
                        sc, cl = ci // 4, ci % 4
                        b = sc % 2
                        par = ci % 2
                        pb0 = (4 + 2 * (ci % 2)) * 512
                        for ph in range(8):
                            w0 = []
                            if ph == 0:
                                w0 = [("pe", sim_pe)]
                                if first_vec[0]:
                                    w0.append(("pool", iota_done))
                                    first_vec[0] = False
                            E("vector", lambda e, ph=ph, pb0=pb0: e.max(
                                out=sxy[:, ph * 8:(ph + 1) * 8],
                                in_=pall[:, pb0 + ph * 128:pb0 + (ph + 1) * 128]),
                                waits=w0)
                            E("vector", lambda e: e.drain())
                            E("vector", lambda e, ph=ph, pb0=pb0: e.max_index(
                                out=ixy[:, ph * 8:(ph + 1) * 8],
                                in_max=sxy[:, ph * 8:(ph + 1) * 8],
                                in_values=pall[:, pb0 + ph * 128:pb0 + (ph + 1) * 128]))
                        E("vector", vdone, inc="dve")
                        E("vector", lambda e: e.tensor_copy(out=ixyf[:], in_=ixy[:]))
                        E("vector", lambda e: e.drain())
                        E("vector", lambda e: e.tensor_scalar(
                            out=posf[:], in0=ixyf[:, 0:32], scalar1=128.0, scalar2=None,
                            op0=OP.mult))
                        E("vector", lambda e: e.drain())
                        E("vector", lambda e: e.tensor_tensor(
                            out=aif[:].rearrange("p (h a b) -> p h a b", h=4, a=8),
                            in0=posf[:].rearrange("p (h a) -> p h a", h=4).unsqueeze(3)
                                .broadcast_to([128, 4, 8, 8]),
                            in1=ixyf[:, 32:64].rearrange("p (h b) -> p h b", h=4).unsqueeze(2)
                                .broadcast_to([128, 4, 8, 8]), op=OP.add))
                        E("vector", lambda e: e.tensor_tensor(
                            out=asf[:, 64:256].rearrange("p (x a b) -> p x a b", x=3, a=8),
                            in0=sxy[:, 8:32].rearrange("p (x a) -> p x a", x=3).unsqueeze(3)
                                .broadcast_to([128, 3, 8, 8]),
                            in1=sxy[:, 40:64].rearrange("p (x b) -> p x b", x=3).unsqueeze(2)
                                .broadcast_to([128, 3, 8, 8]), op=OP.add))
                        E("vector", lambda e: e.tensor_tensor(
                            out=asf[:, 0:64].rearrange("p (a b) -> p a b", a=8),
                            in0=sxy[:, 0:8].unsqueeze(2).broadcast_to([128, 8, 8]),
                            in1=sxy[:, 32:40].unsqueeze(1).broadcast_to([128, 8, 8]),
                            op=OP.add))
                        E("vector", lambda e: e.drain())
                        for h in range(4):
                            E("vector", lambda e, h=h: e.max(
                                out=sc8[:, h * 8:(h + 1) * 8], in_=asf[:, h * 64:(h + 1) * 64]))
                            E("vector", lambda e: e.drain())
                            E("vector", lambda e, h=h: e.max_index(
                                out=posu[:, h * 8:(h + 1) * 8],
                                in_max=sc8[:, h * 8:(h + 1) * 8],
                                in_values=asf[:, h * 64:(h + 1) * 64]))
                        E("vector", lambda e: e.drain())
                        E("vector", lambda e: e.tensor_copy(out=posf[:], in_=posu[:]))
                        E("vector", lambda e: e.drain())
                        for hs in range(32):
                            h = hs // 8
                            E("vector", lambda e, h=h, hs=hs: e.scalar_tensor_tensor(
                                out=sxy[:, 0:64], in0=iota64[:], scalar=posf[:, hs:hs + 1],
                                in1=aif[:, h * 64:(h + 1) * 64], op0=OP.is_equal,
                                op1=OP.mult, accum_out=fidxf[:, hs:hs + 1]))
                        E("vector", lambda e: e.drain())
                        E("vector", lambda e, par=par: e.tensor_copy(
                            out=fidxu2[:, par * 32:(par + 1) * 32], in_=fidxf[:]))
                        E("vector", vdone, inc="dve")
                        fidx_dve_c[ci] = C("dve")
                        if DBGP and ci == 0:
                            E("scalar", lambda e: e.dma_start(
                                out=dbgP[0, :, 0:32], in_=fidxf[:]),
                                waits=[("dve", fidx_dve_c[ci])], inc="dp", inc_n=16,
                                dma=True)
                        E("vector", lambda e: e.tensor_scalar(
                            out=negm4[:].unsqueeze(2),
                            in0=sc8[:].rearrange("p (h a) -> p h a", a=8)[:, :, 0:1],
                            scalar1=-1.0, scalar2=None, op0=OP.mult))
                        E("vector", vdone, inc="dve")
                        neg_dve = C("dve")
                        for h in range(4):
                            E("scalar", lambda e, h=h, par=par: e.activation(
                                out=esc2[:, par * 32 + h * 8:par * 32 + (h + 1) * 8],
                                in_=sc8[:, h * 8:(h + 1) * 8],
                                func=AF.Exp, bias=negm4[:, h:h + 1]),
                                waits=([("dve", neg_dve)] if h == 0 else []), inc="act")
                        esc_act = C("act")
                        E("vector", lambda e, par=par: e.tensor_reduce(
                            out=ssum42[:, par * 8:par * 8 + 4],
                            in_=esc2[:, par * 32:(par + 1) * 32]
                                .rearrange("p (h a) -> p h a", h=4),
                            axis=mybir.AxisListType.X, op=OP.add), waits=[("act", esc_act)])
                        E("vector", lambda e: e.drain())
                        E("vector", lambda e, par=par: e.reciprocal(
                            out=ssum42[:, par * 8 + 4:par * 8 + 8],
                            in_=ssum42[:, par * 8:par * 8 + 4]))
                        E("vector", lambda e: e.drain())
                        E("vector", lambda e, par=par: e.tensor_tensor(
                            out=esc2[:, par * 32:(par + 1) * 32]
                                .rearrange("p (h a) -> p h a", h=4),
                            in0=esc2[:, par * 32:(par + 1) * 32]
                                .rearrange("p (h a) -> p h a", h=4),
                            in1=ssum42[:, par * 8 + 4:par * 8 + 8].unsqueeze(2)
                                .broadcast_to([128, 4, 8]), op=OP.mult))
                        E("vector", vdone, inc="dve")
                        if DBGP and ci == 0:
                            E("scalar", lambda e, par=par: e.dma_start(
                                out=dbgP[1, :, 0:64], in_=esc2[:]),
                                waits=[("dve", C("dve"))], inc="dp", inc_n=16,
                                dma=True)
                            E("scalar", lambda e: e.dma_start(
                                out=dbgP[2, :, 0:16], in_=ssum42[:]),
                                waits=[("dve", C("dve"))], inc="dp", inc_n=16,
                                dma=True)

                    def emit_gather_phase(ci, st, fillers):
                        sc, cl = ci // 4, ci % 4
                        b = sc % 2
                        par = ci % 2
                        t0c = ci * 128
                        eb = 4 + 2 * par
                        gbase = ci * 8
                        fi = [0]

                        def pop_filler():
                            if fi[0] < len(fillers):
                                fillers[fi[0]]()
                                fi[0] += 1
                        for oc in range(2):
                            E("tensor", lambda e, oc=oc, eb=eb: e.matmul(
                                out=bank(eb + oc), lhsT=onesr[:, 0:128], rhs=zrow[:],
                                start=True, stop=False, skip_group_check=True),
                                waits=([("act", C("act")), ("dve", C("dve"))]
                                       if oc == 0 else []))
                        ttc = {}
                        sic = {}
                        ccc = {}
                        dgc = {}
                        for phs in range(11):
                            pop_filler()
                            if 3 <= phs <= 10:
                                q = phs - 3
                                for sl in range(4):
                                    hs = q * 4 + sl
                                    tl = (ci * 32 + hs) % NT
                                    for oc in range(2):
                                        w0 = ([("dve", dgc[q])]
                                              if (sl == 0 and oc == 0) else [])
                                        E("tensor", lambda e, sl=sl, tl=tl, oc=oc,
                                            hs=hs, eb=eb, q=q: e.matmul(
                                            out=bank(eb + oc),
                                            lhsT=diag8[:, (q % 2) * 512 + sl * 128:
                                                       (q % 2) * 512 + (sl + 1) * 128],
                                            rhs=gt[tl][:, 1024 + oc * 512:1024 + (oc + 1) * 512],
                                            start=False, stop=(hs == 31),
                                            skip_group_check=True), waits=w0,
                                            inc=("pe" if oc == 1 else None))
                                    gt_free[tl] = C("pe")
                                mmc[gbase + q] = C("pe")
                            if 2 <= phs <= 9:
                                q = phs - 2
                                E("scalar", lambda e, q=q: e.activation(
                                    out=sw32[:, q * 4:(q + 1) * 4],
                                    in_=wv32[:, q * 4:(q + 1) * 4],
                                    func=AF.Silu),
                                    waits=([] if DOTMODE == "actaccum"
                                           else [("dve", ttc[q])]), inc="act")
                                sic[q] = C("act")
                                E("vector", lambda e, q=q, par=par: e.tensor_tensor(
                                    out=ccs[:, q * 4:(q + 1) * 4],
                                    in0=sw32[:, q * 4:(q + 1) * 4],
                                    in1=esc2[:, par * 32 + q * 4:par * 32 + (q + 1) * 4],
                                    op=OP.mult), waits=[("act", sic[q])])
                                E("vector", lambda e: e.drain())
                                wdg = []
                                if gbase + q - 2 in mmc:
                                    wdg = [("pe", mmc[gbase + q - 2])]
                                for sl in range(4):
                                    hs = q * 4 + sl
                                    E("vector", lambda e, sl=sl, hs=hs, q=q: e.tensor_scalar(
                                        out=diag8[:, (q % 2) * 512 + sl * 128:
                                                   (q % 2) * 512 + (sl + 1) * 128],
                                        in0=eyebf[:], scalar1=ccs[:, hs:hs + 1],
                                        scalar2=None, op0=OP.mult),
                                        waits=(wdg if sl == 0 else []))
                                E("vector", vdone, inc="dve")
                                dgc[q] = C("dve")
                            if 1 <= phs <= 8 and DOTMODE == "actaccum":
                                q = phs - 1
                                for sl in range(4):
                                    hs = q * 4 + sl
                                    tl = (ci * 32 + hs) % NT
                                    E("scalar", lambda e, tl=tl, hs=hs: e.activation(
                                        out=gt[tl][:, 0:1024], in_=gt[tl][:, 0:1024],
                                        func=AF.Copy, accum_out=wv32[:, hs:hs + 1]),
                                        waits=([("dve", ttc[q])] if sl == 0 else []))
                            if phs < 8:
                                q = phs
                                for sl in range(4):
                                    hs = q * 4 + sl
                                    tl = (ci * 32 + hs) % NT
                                    gsem = f"g{tl}"
                                    wl = [("dve", fidx_dve_c[ci])]
                                    if tl in gt_free:
                                        wl.append(("pe", gt_free[tl]))
                                    E("gpsimd", lambda e, hs=hs, tl=tl, par=par:
                                        e.indirect_dma_start(
                                            out=gt[tl][:], out_offset=None, in_=tbl2,
                                            in_offset=bass.IndirectOffsetOnAxis(
                                                ap=fidxu2[:, par * 32 + hs:par * 32 + hs + 1],
                                                axis=0)),
                                        waits=wl, inc=gsem, inc_n=16, dma=True)
                                    gld = C(gsem)
                                    wt = [(gsem, gld)]
                                    if hs == 0:
                                        wt.append((st["tsem"], st["xnt_ld"]))
                                    if DOTMODE == "actaccum":
                                        E("vector", lambda e, tl=tl, b=b, cl=cl:
                                            e.tensor_tensor(
                                            out=gt[tl][:, 0:1024], in0=gt[tl][:, 0:1024],
                                            in1=xnt2[b][:, cl * 1024:(cl + 1) * 1024],
                                            op=OP.mult), waits=wt)
                                        if sl == 3:
                                            E("vector", vdone, inc="dve")
                                    else:
                                        E("vector", lambda e, tl=tl, b=b, cl=cl, hs=hs:
                                            e.scalar_tensor_tensor(
                                            out=junkb[:], in0=gt[tl][:, 0:1024],
                                            scalar=1.0,
                                            in1=xnt2[b][:, cl * 1024:(cl + 1) * 1024],
                                            op0=OP.mult, op1=OP.mult,
                                            accum_out=wv32[:, hs:hs + 1]), waits=wt)
                                        if sl == 3:
                                            E("vector", vdone, inc="dve")
                                ttc[q] = C("dve")
                        while fi[0] < len(fillers):
                            pop_filler()
                        if DBGP and ci == 0:
                            E("scalar", lambda e: e.dma_start(
                                out=dbgP[3, :, 0:32], in_=wv32[:]),
                                waits=[("dve", ttc[7])], inc="dp", inc_n=16, dma=True)
                            E("scalar", lambda e: e.dma_start(
                                out=dbgP[4, :, 0:32], in_=ccs[:]),
                                waits=[("dve", dgc[7])], inc="dp", inc_n=16, dma=True)
                            E("scalar", lambda e: e.dma_start(
                                out=dbgP[5, :, 0:32], in_=sw32[:]),
                                waits=[("dve", dgc[7])], inc="dp", inc_n=16, dma=True)
                        E("vector", lambda e, eb=eb: e.tensor_copy(out=expt[:], in_=bank(eb, 2)),
                          waits=[("pe", C("pe")), ("se", exp_free)])
                        E("vector", vdone, inc="dve")
                        E("scalar", lambda e, t0c=t0c: e.dma_start(
                            out=exp_d[t0c:t0c + 128, :], in_=expt[:]),
                            waits=[("dve", C("dve"))], inc="se", inc_n=16, dma=True)
                        return C("se")

                    st_by_sc = {0: emit_schead_a(0)}
                    emit_qf_wave(0, 0, st_by_sc[0])
                    emit_qf_wave(0, 1, st_by_sc[0])
                    for ci in range(-1, NCH):
                        nci = ci + 1
                        fillers = []
                        if nci < NCH:
                            nsc = nci // 4
                            if nci % 4 == 1 and nsc + 1 < NSC:
                                st_by_sc[nsc + 1] = emit_schead_a(nsc + 1)
                                emit_qf_wave(nsc + 1, 0, st_by_sc[nsc + 1])
                            if nci % 4 == 2 and nsc + 1 < NSC:
                                emit_qf_wave(nsc + 1, 1, st_by_sc[nsc + 1])
                            fillers += p2_piece_units(nsc, nci % 4, st_by_sc[nsc])
                            sim_pe = emit_sim(nci, st_by_sc[nsc])
                            emit_topk(nci, st_by_sc[nsc], sim_pe)
                        if ci >= 0:
                            exp_free = emit_gather_phase(ci, st_by_sc[ci // 4], fillers)
                        else:
                            for u in fillers:
                                u()

        E("sync", lambda e: e.nop(), waits=rec.all_dma_waits(), inc="bar")
        E("scalar", lambda e: e.nop(), waits=[("act", C("act"))], inc="bar")
        E("vector", vdone, waits=[("dve", C("dve"))], inc="bar")
        E("tensor", lambda e: e.nop(), waits=[("pe", C("pe"))], inc="bar")
        E("gpsimd", lambda e: e.nop(), waits=[("pool", C("pool"))], inc="bar")
        bar2 = C("bar")

        if "p3" in PHASES:
            rec.scope = "p3"
            with ExitStack() as c3:
                wgT = c3.enter_context(nc.sbuf_tensor("wgT", [128, 16384], BF16))
                wuT = c3.enter_context(nc.sbuf_tensor("wuT", [128, 16384], BF16))
                wdT = c3.enter_context(nc.sbuf_tensor("wdT", [128, 16384], BF16))
                xnf3 = [c3.enter_context(nc.sbuf_tensor(f"xnf3_{i}", [128, 4096], BF16))
                        for i in range(2)]
                sg = [c3.enter_context(nc.sbuf_tensor(f"sg{i}", [128, 512], BF16))
                      for i in range(2)]
                inter2 = [c3.enter_context(nc.sbuf_tensor(f"inter{i}", [128, 8192], BF16))
                          for i in range(2)]
                m1t = [c3.enter_context(nc.sbuf_tensor(f"m1t{i}", [128, 1024], BF16))
                       for i in range(2)]
                sldt = [c3.enter_context(nc.sbuf_tensor(f"sldt{i}", [128, 1024], BF16))
                        for i in range(2)]
                xldt = [c3.enter_context(nc.sbuf_tensor(f"xldt{i}", [128, 1024], F32))
                        for i in range(2)]
                eldt = [c3.enter_context(nc.sbuf_tensor(f"eldt{i}", [128, 1024], BF16))
                        for i in range(2)]
                cmbt = [c3.enter_context(nc.sbuf_tensor(f"cmb{i}", [128, 1024], F32))
                        for i in range(2)]
                outt = [c3.enter_context(nc.sbuf_tensor(f"outt{i}", [128, 1024], F32))
                        for i in range(2)]

                m1_free = {}
                out_free = {}
                mlp_bank_free = {}
                half0_stores = []
                for half in range(2):
                    rec.scope = f"p3_{half}"
                    h0 = half * 2048
                    for ic in range(8):
                        E("sync", lambda e, ic=ic, h0=h0: e.dma_start_transpose(
                            out=wgT[:, ic * 2048:(ic + 1) * 2048],
                            in_=wgbf[h0:h0 + 2048, ic * 128:(ic + 1) * 128]),
                            waits=([("bar", bar2), ("pe", C("pe"))] if ic == 0 else []),
                            inc="wl3", inc_n=16, dma=True)
                        E("sync", lambda e, ic=ic, h0=h0: e.dma_start_transpose(
                            out=wuT[:, ic * 2048:(ic + 1) * 2048],
                            in_=wubf[h0:h0 + 2048, ic * 128:(ic + 1) * 128]),
                            inc="wl3", inc_n=16, dma=True)
                    for it in range(16):
                        E("sync", lambda e, it=it, h0=h0: e.dma_start_transpose(
                            out=wdT[:, it * 1024:(it + 1) * 1024],
                            in_=wdbf[:, h0 + it * 128:h0 + it * 128 + 128]),
                            inc="wl3", inc_n=16, dma=True)
                    wld = C("wl3")

                    wd_pe_prev = {}
                    for sc in range(NSC):
                        b = sc % 2
                        t0 = sc * 512
                        xsem = f"xnf{b}"
                        itb = inter2[sc % 2]
                        for ic in range(8):
                            E("sync", lambda e, t0=t0, ic=ic, b=b: e.dma_start_transpose(
                                out=xnf3[b][:, ic * 512:(ic + 1) * 512],
                                in_=xn_d[t0:t0 + 512, ic * 128:(ic + 1) * 128]),
                                waits=([("pe", C("pe"))] if ic == 0 else []),
                                inc=xsem, inc_n=16, dma=True)
                        xn_ld = C(xsem)
                        silu_cnt = {}
                        tt_cnt = {}
                        for icc in range(16):
                            bg = (icc % 2) * 2
                            bu = bg + 1
                            for ic in range(8):
                                w0 = []
                                if ic == 0:
                                    w0 = [(xsem, xn_ld), ("wl3", wld)]
                                    if icc - 2 in silu_cnt:
                                        w0 += [("act", silu_cnt[icc - 2]),
                                               ("dve", tt_cnt[icc - 2])]
                                    elif icc < 2 and sc == 0 and half == 0:
                                        w0 += [("dve", C("dve")), ("act", C("act"))]
                                    elif icc < 2:
                                        w0 += [("act", C("act")), ("dve", C("dve"))]
                                    if icc == 0:
                                        w0.append(("pe", C("pe")))
                                E("tensor", lambda e, b=b, ic=ic, icc=icc, bg=bg: e.matmul(
                                    out=bank(bg),
                                    lhsT=wgT[:, ic * 2048 + icc * 128:ic * 2048 + icc * 128 + 128],
                                    rhs=xnf3[b][:, ic * 512:(ic + 1) * 512],
                                    start=(ic == 0), stop=(ic == 7)), waits=w0)
                            for ic in range(8):
                                E("tensor", lambda e, b=b, ic=ic, icc=icc, bu=bu: e.matmul(
                                    out=bank(bu),
                                    lhsT=wuT[:, ic * 2048 + icc * 128:ic * 2048 + icc * 128 + 128],
                                    rhs=xnf3[b][:, ic * 512:(ic + 1) * 512],
                                    start=(ic == 0), stop=(ic == 7)),
                                    inc=("pe" if ic == 7 else None))
                            iu_pe = C("pe")
                            sgt = sg[icc % 2]
                            E("scalar", lambda e, bg=bg, sgt=sgt: e.activation(
                                out=sgt[:], in_=bank(bg), func=AF.Silu),
                                waits=[("pe", iu_pe), ("dve", C("dve"))], inc="act")
                            silu_cnt[icc] = C("act")
                            ttw = [("act", C("act"))]
                            if icc == 0 and (sc % 2) in wd_pe_prev:
                                ttw.append(("pe", wd_pe_prev[sc % 2]))
                            E("vector", lambda e, bu=bu, icc=icc, sgt=sgt, itb=itb:
                                e.tensor_tensor(
                                    out=itb[:, icc * 512:(icc + 1) * 512], in0=bank(bu),
                                    in1=sgt[:], op=OP.mult), waits=ttw)
                            E("vector", vdone, inc="dve")
                            tt_cnt[icc] = C("dve")
                        int_dve = C("dve")
                        for tsub in range(4):
                            t0c = t0 + tsub * 128
                            bb = tsub % 2
                            csem = f"cmb{bb}"
                            if half == 1:
                                wl = [("dve", C("dve"))] + half0_stores
                                E("sync", lambda e, t0c=t0c, bb=bb: e.dma_start(
                                    out=m1t[bb][:], in_=mlp1_d[t0c:t0c + 128, :]),
                                    waits=wl, inc=csem, inc_n=16, dma=True)
                                E("sync", lambda e, t0c=t0c, bb=bb: e.dma_start(
                                    out=sldt[bb][:], in_=s_d[t0c:t0c + 128, :]),
                                    inc=csem, inc_n=16, dma=True)
                                E("sync", lambda e, t0c=t0c, bb=bb: e.dma_start(
                                    out=xldt[bb][:], in_=x_d[t0c:t0c + 128, :]),
                                    inc=csem, inc_n=16, dma=True)
                                E("sync", lambda e, t0c=t0c, bb=bb: e.dma_start(
                                    out=eldt[bb][:], in_=exp_d[t0c:t0c + 128, :]),
                                    inc=csem, inc_n=16, dma=True)
                            cmb_ld = C(csem)
                            mb0 = 4 + 2 * (tsub % 2) if half == 1 else 4
                            for oc in range(2):
                                for icc in range(16):
                                    w0 = []
                                    if oc == 0 and icc == 0:
                                        w0 = [("dve", int_dve), ("act", C("act"))]
                                        if half == 1 and (tsub % 2) in mlp_bank_free:
                                            w0.append(("dve", mlp_bank_free[tsub % 2]))
                                    E("tensor", lambda e, icc=icc, oc=oc, tsub=tsub,
                                        itb=itb, mb0=mb0: e.matmul(
                                        out=bank(mb0 + oc),
                                        lhsT=itb[:, icc * 512 + tsub * 128:icc * 512 + tsub * 128 + 128],
                                        rhs=wdT[:, icc * 1024 + oc * 512:icc * 1024 + (oc + 1) * 512],
                                        start=(icc == 0), stop=(icc == 15)), waits=w0,
                                        inc=("pe" if (icc == 15 and oc == 1) else None))
                            mlp_pe = C("pe")
                            if half == 0:
                                msem = f"sm{bb}"
                                wse = [("pe", mlp_pe)]
                                if msem in m1_free:
                                    wse.append((msem, m1_free[msem]))
                                E("scalar", lambda e, bb=bb: e.activation(
                                    out=m1t[bb][:, 0:512], in_=bank(4), func=AF.Copy),
                                    waits=wse)
                                E("scalar", lambda e, bb=bb: e.activation(
                                    out=m1t[bb][:, 512:1024], in_=bank(5), func=AF.Copy),
                                    inc="act")
                                E("scalar", lambda e, t0c=t0c, bb=bb: e.dma_start(
                                    out=mlp1_d[t0c:t0c + 128, :], in_=m1t[bb][:]),
                                    waits=[("act", C("act"))], inc=msem, inc_n=16, dma=True)
                                m1_free[msem] = C(msem)
                                E("tensor", lambda e: e.nop(), waits=[("act", C("act"))],
                                  inc="pe")
                            else:
                                cb = cmbt[bb]
                                ot = outt[bb]
                                E("vector", lambda e, cb=cb, bb=bb, mb0=mb0: e.tensor_tensor(
                                    out=cb[:, 0:512], in0=bank(mb0), in1=m1t[bb][:, 0:512],
                                    op=OP.add), waits=[("pe", mlp_pe), (csem, cmb_ld)])
                                E("vector", lambda e, cb=cb, bb=bb, mb0=mb0: e.tensor_tensor(
                                    out=cb[:, 512:1024], in0=bank(mb0 + 1),
                                    in1=m1t[bb][:, 512:1024], op=OP.add))
                                E("vector", lambda e: e.drain())
                                E("vector", lambda e, cb=cb, bb=bb: e.tensor_tensor(
                                    out=cb[:], in0=cb[:], in1=eldt[bb][:], op=OP.add))
                                E("vector", vdone, inc="dve")
                                mlp_bank_free[tsub % 2] = C("dve")
                                E("scalar", lambda e, cb=cb: e.activation(
                                    out=cb[:], in_=cb[:], func=AF.Silu),
                                    waits=[("dve", C("dve"))], inc="act")
                                E("vector", lambda e, cb=cb, bb=bb: e.tensor_tensor(
                                    out=cb[:], in0=cb[:], in1=sldt[bb][:], op=OP.mult),
                                    waits=[("act", C("act"))])
                                E("vector", lambda e: e.drain())
                                osem = f"so{bb}"
                                wl = [] if osem not in out_free else [(osem, out_free[osem])]
                                E("vector", lambda e, cb=cb, ot=ot, bb=bb: e.tensor_tensor(
                                    out=ot[:], in0=cb[:], in1=xldt[bb][:], op=OP.add),
                                    waits=wl)
                                E("vector", vdone, inc="dve")
                                E("scalar", lambda e, t0c=t0c, ot=ot: e.dma_start(
                                    out=out_d[t0c:t0c + 128, :], in_=ot[:]),
                                    waits=[("dve", C("dve"))], inc=osem, inc_n=16, dma=True)
                                out_free[osem] = C(osem)
                                E("tensor", lambda e: e.nop(), waits=[("dve", C("dve"))],
                                  inc="pe")
                        wd_pe_prev[sc % 2] = C("pe")
                    if half == 0:
                        half0_stores = [(s_, rec.c(s_)) for s_ in ("sm0", "sm1")
                                        if rec.c(s_) > 0]

        if DEBUG and dbg_d is not None:
            srcs = [xn_d, s_d, exp_d, mlp1_d][:n_dbg]
            for i, sd in enumerate(srcs):
                E("gpsimd", lambda e, i=i, sd=sd: e.dma_start(out=dbg_d[i], in_=sd),
                  waits=rec.all_dma_waits(), inc="dbgc", inc_n=16, dma=True)

        E("sync", lambda e: e.nop(), waits=rec.all_dma_waits(), inc="bar")

        names = set(rec.cnt)
        for o in rec.ops:
            for (sname, v) in o["waits"]:
                names.add(sname)
        sems = {n: en(nc.semaphore(n)) for n in sorted(names)}
        block = en(nc.Block())

        def runner(name):
            def _run(e):
                cur = None
                cm = None
                for o in rec.ops:
                    if o["eng"] != name:
                        continue
                    if o.get("scope") != cur:
                        if cm is not None:
                            cm.__exit__(None, None, None)
                        cur = o.get("scope")
                        cm = nc.named_scope(cur)
                        cm.__enter__()
                    for (sname, v) in o["waits"]:
                        e.wait_ge(sems[sname], v)
                    ins = o["fn"](e)
                    if o["inc"] is not None:
                        sname, n = o["inc"]
                        ins.then_inc(sems[sname], n)
                if cm is not None:
                    cm.__exit__(None, None, None)
            return _run

        block.sync(runner("sync"))
        block.gpsimd(runner("gpsimd"))
        block.vector(runner("vector"))
        block.scalar(runner("scalar"))
        block.tensor(runner("tensor"))

    return nc


def make_in_maps(inputs, n_cores=N_CORES):
    x = np.ascontiguousarray(inputs["L0_attn_x"])
    bias5 = np.ascontiguousarray(np.stack([
        inputs["L0_attn_bq"], inputs["L0_attn_bk"], inputs["L0_attn_bv"],
        inputs["L0_attn_bg"], inputs["L0_attn_bo"]], 0))
    base = dict(
        wq=np.ascontiguousarray(inputs["L0_attn_Wq"]),
        wk=np.ascontiguousarray(inputs["L0_attn_Wk"]),
        wv=np.ascontiguousarray(inputs["L0_attn_Wv"]),
        wg=np.ascontiguousarray(inputs["L0_attn_Wg"]),
        wo=np.ascontiguousarray(inputs["L0_attn_Wo"]),
        bias5=bias5,
        wqf=np.ascontiguousarray(inputs["L0_ffn_Wq"]),
        keys=np.ascontiguousarray(inputs["L0_ffn_keys"]),
        down=np.ascontiguousarray(inputs["L0_ffn_down_embed"]),
        up=np.ascontiguousarray(inputs["L0_ffn_up_embed"]),
        wgf=np.ascontiguousarray(inputs["L0_ffn_Wg"]),
        wuf=np.ascontiguousarray(inputs["L0_ffn_Wu"]),
        wdf=np.ascontiguousarray(inputs["L0_ffn_Wd"]),
        eye=np.eye(128, dtype=np.float32),
    )
    return [dict(base, x=np.ascontiguousarray(x[c])) for c in range(n_cores)]


def kernel(**inputs):
    in_maps = make_in_maps(inputs)
    nc = build()
    res = run_bass_kernel_spmd(nc, in_maps, core_ids=list(range(N_CORES)))
    return np.stack([r["out"] for r in res.results], 0)


# version 14 (final)
# speedup vs baseline: 1.1837x; 1.1837x over previous
import numpy as np
from contextlib import ExitStack
import concourse.bass as bass
import concourse.mybir as mybir
from concourse.bass_utils import run_bass_kernel_spmd
from concourse.alu_op_type import AluOpType as OP

F32 = mybir.dt.float32
BF16 = mybir.dt.bfloat16
U32 = mybir.dt.uint32
AF = mybir.ActivationFunctionType

S = 4096
D = 1024
NCH = 32
DBGP = 0
DOTMODE = "actaccum"
NSC = 8
NEXP = 16384
INTER = 4096

DEBUG = 0
N_CORES = 8
PHASES = ("p1a", "p1b", "p2", "peer", "p3")


class Rec:
    def __init__(self):
        self.ops = []
        self.cnt = {}
        self.dma_sems = set()

    def c(self, s):
        return self.cnt.get(s, 0)

    scope = "init"

    def emit(self, eng, fn, waits=(), inc=None, inc_n=1, dma=False):
        item = dict(eng=eng, fn=fn, waits=[(s, v) for (s, v) in waits if v > 0],
                    inc=None, scope=self.scope)
        if inc is not None:
            self.cnt[inc] = self.c(inc) + inc_n
            item["inc"] = (inc, inc_n)
            if dma:
                self.dma_sems.add(inc)
        self.ops.append(item)

    def all_dma_waits(self):
        return [(s, self.c(s)) for s in sorted(self.dma_sems)]


def vdone(e):
    e.drain()
    return e.nop()


def build(n_dbg=0):
    nc = bass.Bass("TRN2", target_bir_lowering=False)

    x_d = nc.dram_tensor("x", [S, D], F32, kind="ExternalInput").ap()
    wq_d = nc.dram_tensor("wq", [D, D], F32, kind="ExternalInput").ap()
    wk_d = nc.dram_tensor("wk", [D, D], F32, kind="ExternalInput").ap()
    wv_d = nc.dram_tensor("wv", [D, D], F32, kind="ExternalInput").ap()
    wg_d = nc.dram_tensor("wg", [D, D], F32, kind="ExternalInput").ap()
    wo_d = nc.dram_tensor("wo", [D, D], F32, kind="ExternalInput").ap()
    bias5_d = nc.dram_tensor("bias5", [5, D], F32, kind="ExternalInput").ap()
    wqf_d = nc.dram_tensor("wqf", [512, D], F32, kind="ExternalInput").ap()
    keys_d = nc.dram_tensor("keys", [4, 128, 2, 64], F32, kind="ExternalInput").ap()
    down_d = nc.dram_tensor("down", [NEXP, D], F32, kind="ExternalInput").ap()
    up_d = nc.dram_tensor("up", [NEXP, D], F32, kind="ExternalInput").ap()
    wgf_d = nc.dram_tensor("wgf", [INTER, D], F32, kind="ExternalInput").ap()
    wuf_d = nc.dram_tensor("wuf", [INTER, D], F32, kind="ExternalInput").ap()
    wdf_d = nc.dram_tensor("wdf", [D, INTER], F32, kind="ExternalInput").ap()
    eye_d = nc.dram_tensor("eye", [128, 128], F32, kind="ExternalInput").ap()

    out_d = nc.dram_tensor("out", [S, D], F32, kind="ExternalOutput").ap()
    dbg_d = None
    if n_dbg:
        dbg_d = nc.dram_tensor("dbg", [n_dbg, S, D], BF16, kind="ExternalOutput").ap()
    dbgP = None
    if DBGP:
        dbgP = nc.dram_tensor("dbgP", [6, 128, 64], F32, kind="ExternalOutput").ap()

    tbl2 = nc.dram_tensor("tbl2", [NEXP, 2 * D], BF16).ap()
    wgbf = nc.dram_tensor("wgbf", [INTER, D], BF16).ap()
    wubf = nc.dram_tensor("wubf", [INTER, D], BF16).ap()
    wdbf = nc.dram_tensor("wdbf", [D, INTER], BF16).ap()
    wqfbf = nc.dram_tensor("wqfbf", [512, D], BF16).ap()
    bias_bf = nc.dram_tensor("bias_bf", [5, D], BF16).ap()
    xn_d = nc.dram_tensor("xn_td", [S, D], BF16).ap()
    s_d = nc.dram_tensor("s_td", [S, D], BF16).ap()
    exp_d = nc.dram_tensor("exp_td", [S, D], BF16).ap()
    mlp1_d = nc.dram_tensor("mlp1_td", [S, D], BF16).ap()

    rec = Rec()
    E = rec.emit
    C = rec.c

    with ExitStack() as ctx:
        en = ctx.enter_context

        eyebf = en(nc.sbuf_tensor("eyebf", [128, 128], BF16))
        onesr = en(nc.sbuf_tensor("onesr", [1, 512], BF16))
        zrow = en(nc.sbuf_tensor("zrow", [1, 512], BF16))
        iota64 = en(nc.sbuf_tensor("iota64", [128, 64], F32))
        epsv = en(nc.sbuf_tensor("epsv", [128, 1], F32))
        brow = en(nc.sbuf_tensor("brow", [1, 5 * D], BF16))
        kvsb = en(nc.sbuf_tensor("kvsb", [128, 512], BF16))
        wqfT = en(nc.sbuf_tensor("wqfT", [128, 8 * 512], BF16))
        keysT = en(nc.sbuf_tensor("keysT", [64, 8 * 128], BF16))
        pall = en(nc.psum_tensor("pall", [128, 4096], F32))

        def bank(b, n=1):
            return pall[:, 512 * b:512 * (b + n)]

        for i in range(2):
            E("gpsimd", lambda e, i=i: e.dma_start(
                out=wgbf[i * 2048:(i + 1) * 2048, :],
                in_=wgf_d[i * 2048:(i + 1) * 2048, :]), inc="cast", inc_n=16, dma=True)
            E("gpsimd", lambda e, i=i: e.dma_start(
                out=wubf[i * 2048:(i + 1) * 2048, :],
                in_=wuf_d[i * 2048:(i + 1) * 2048, :]), inc="cast", inc_n=16, dma=True)
            E("gpsimd", lambda e, i=i: e.dma_start(
                out=wdbf[i * 512:(i + 1) * 512, :],
                in_=wdf_d[i * 512:(i + 1) * 512, :]), inc="cast", inc_n=16, dma=True)
        E("gpsimd", lambda e: e.dma_start(out=wqfbf, in_=wqf_d),
          inc="cast", inc_n=16, dma=True)
        cast_done = C("cast")
        for i in range(4):
            E("gpsimd", lambda e, i=i: e.dma_start(
                out=tbl2[i * 4096:(i + 1) * 4096, 0:D],
                in_=down_d[i * 4096:(i + 1) * 4096, :]), inc="castT", inc_n=16, dma=True)
        for i in range(4):
            E("gpsimd", lambda e, i=i: e.dma_start(
                out=tbl2[i * 4096:(i + 1) * 4096, D:2 * D],
                in_=up_d[i * 4096:(i + 1) * 4096, :]), inc="castT", inc_n=16, dma=True)
        cast_doneT = C("castT")

        E("gpsimd", lambda e: e.iota(iota64[:], [[1, 64]], channel_multiplier=0,
                                     allow_small_or_imprecise_dtypes=True), inc="pool")
        iota_done = C("pool")

        E("vector", lambda e: e.memset(onesr[:], 1.0))
        E("vector", lambda e: e.memset(zrow[:], 0.0))
        E("vector", lambda e: e.memset(epsv[:], 1e-5))
        E("vector", vdone, inc="dve")
        consts_done = C("dve")

        with ExitStack() as c1:
            wqs = c1.enter_context(nc.sbuf_tensor("wqs", [128, 8192], BF16))
            wgs = c1.enter_context(nc.sbuf_tensor("wgs", [128, 8192], BF16))
            wos = c1.enter_context(nc.sbuf_tensor("wos", [128, 8192], BF16))
            xnf = [c1.enter_context(nc.sbuf_tensor(f"xnf{i}", [128, 4096], BF16))
                   for i in range(2)]
            esb = c1.enter_context(nc.sbuf_tensor("esb", [128, 512], BF16))
            rsb = c1.enter_context(nc.sbuf_tensor("rsb", [128, 512], BF16))
            qps = c1.enter_context(nc.sbuf_tensor("qps", [128, 4096], BF16))
            gts = c1.enter_context(nc.sbuf_tensor("gts", [128, 4096], BF16))
            gas = c1.enter_context(nc.sbuf_tensor("gas", [128, 4096], BF16))
            ssb = [c1.enter_context(nc.sbuf_tensor(f"ssb{i}", [128, 1024], BF16))
                   for i in range(2)]
            s1 = ExitStack()
            wks = s1.enter_context(nc.sbuf_tensor("wks", [128, 8192], BF16))
            wvs = s1.enter_context(nc.sbuf_tensor("wvs", [128, 8192], BF16))
            kpsb = s1.enter_context(nc.sbuf_tensor("kpsb", [128, 1024], BF16))
            vpsb = s1.enter_context(nc.sbuf_tensor("vpsb", [128, 1024], BF16))
            s2 = ExitStack()
            wstg = [s2.enter_context(nc.sbuf_tensor(f"wstg{i}", [128, 1024], F32))
                    for i in range(2)]
            xt = [s2.enter_context(nc.sbuf_tensor(f"xt{i}", [128, 1024], F32))
                  for i in range(2)]
            xnt1 = [s2.enter_context(nc.sbuf_tensor(f"xnt1_{i}", [128, 1024], BF16))
                    for i in range(2)]
            stats = s2.enter_context(nc.sbuf_tensor("stats", [128, 12], F32))
            mv = [s2.enter_context(nc.sbuf_tensor(f"mv{i}", [128, 2], F32)) for i in range(2)]
            rst = [s2.enter_context(nc.sbuf_tensor(f"rst{i}", [128, 2], F32)) for i in range(2)]
            kstage = s2.enter_context(nc.sbuf_tensor("kstage", [128, 64], F32))
            eye = s2.enter_context(nc.sbuf_tensor("eyef", [128, 128], F32))

            E("sync", lambda e: e.dma_start(out=eye[:], in_=eye_d),
              inc="ldc", inc_n=16, dma=True)
            ld_eye = C("ldc")
            E("vector", lambda e: e.tensor_copy(out=eyebf[:], in_=eye[:]),
              waits=[("ldc", ld_eye)])
            E("vector", vdone, inc="dve")
            eyebf_done = C("dve")
            E("gpsimd", lambda e: e.dma_start(out=bias_bf, in_=bias5_d),
              inc="ldb", inc_n=16, dma=True)
            E("sync", lambda e: e.dma_start(
                out=brow[:], in_=bias_bf.rearrange("a b -> (a b)").unsqueeze(0)),
                waits=[("ldb", C("ldb"))], inc="ldb2", inc_n=16, dma=True)
            E("gpsimd", lambda e: e.nop(), waits=[("ldb2", C("ldb2"))], inc="pool")

            wconv = []
            for wsb, wd in ((wks, wk_d), (wvs, wv_d), (wqs, wq_d), (wgs, wg_d), (wos, wo_d)):
                for ic in range(8):
                    wconv.append((wsb, wd, ic))
            wstg_free = {}

            def emit_wconv(j):
                wsb, wd, ic = wconv[j]
                stg = wstg[j % 2]
                sem = f"ldw{j % 2}"
                wl = [("pool", wstg_free[j - 2])] if j - 2 in wstg_free else []
                E("sync", lambda e, stg=stg, wd=wd, ic=ic: e.dma_start(
                    out=stg[:], in_=wd[ic * 128:(ic + 1) * 128, :]), waits=wl,
                    inc=sem, inc_n=16, dma=True)
                xv = C(sem)
                E("gpsimd", lambda e, wsb=wsb, stg=stg, ic=ic: e.tensor_copy(
                    out=wsb[:, ic * 1024:(ic + 1) * 1024], in_=stg[:]),
                    waits=[(sem, xv)], inc="pool")
                wstg_free[j] = C("pool")

            def emit_keys():
                for ph in range(8):
                    p_, h_ = ph // 4, ph % 4
                    E("sync", lambda e, h_=h_, p_=p_: e.dma_start(
                        out=kstage[:, 0:64], in_=keys_d[h_, :, p_, :]),
                        waits=[("act", C("act"))], inc="ldk", inc_n=16, dma=True)
                    kl = C("ldk")
                    E("tensor", lambda e: e.transpose(
                        out=pall[0:64, 0:128], in_=kstage[:], identity=eye[:]),
                        waits=[("ldk", kl), ("ldc", ld_eye), ("act", C("act"))], inc="pe")
                    kp = C("pe")
                    E("scalar", lambda e, ph=ph: e.activation(
                        out=keysT[0:64, ph * 128:(ph + 1) * 128],
                        in_=pall[0:64, 0:128], func=AF.Copy), waits=[("pe", kp)],
                        inc="act")

            def emit_wqft():
                for ic in range(8):
                    E("sync", lambda e, ic=ic: e.dma_start_transpose(
                        out=wqfT[:, ic * 512:(ic + 1) * 512],
                        in_=wqfbf[:, ic * 128:(ic + 1) * 128]),
                        waits=([("cast", cast_done)] if ic == 0 else []),
                        inc="wqf", inc_n=16, dma=True)

            rec.scope = "p1a"
            p1a_st = {}
            p1a_x_act = {}
            if "p1a" in PHASES:
                for ci in range(NCH):
                    b = ci % 2
                    t0 = ci * 128
                    sx = f"ldx{b}"
                    wl = [("act", p1a_x_act[ci - 2])] if ci - 2 in p1a_x_act else []
                    E("sync", lambda e, t0=t0, b=b: e.dma_start(
                        out=xt[b][:], in_=x_d[t0:t0 + 128, :]), waits=wl,
                        inc=sx, inc_n=16, dma=True)
                    xld = C(sx)
                    E("vector", lambda e, b=b: e.bn_stats(out=stats[:, 0:6],
                                                          in_=xt[b][:, 0:512]),
                      waits=[(sx, xld), ("dve", consts_done)])
                    E("vector", lambda e, b=b: e.bn_stats(out=stats[:, 6:12],
                                                          in_=xt[b][:, 512:1024]))
                    E("vector", lambda e: e.drain())
                    E("vector", lambda e, b=b: e.bn_aggr(out=mv[b][:], in_=stats[:]))
                    E("vector", vdone, inc="dve")
                    E("scalar", lambda e, b=b: e.activation(
                        out=rst[b][:, 0:1], in_=mv[b][:, 1:2], func=AF.Sqrt, bias=epsv[:]),
                        waits=[("dve", C("dve"))], inc="act")
                    E("vector", lambda e, b=b: e.reciprocal(out=rst[b][:, 1:2],
                                                            in_=rst[b][:, 0:1]),
                      waits=[("act", C("act"))])
                    E("vector", lambda e: e.drain())
                    E("vector", lambda e, b=b: e.tensor_scalar(
                        out=rst[b][:, 0:1], in0=mv[b][:, 0:1], scalar1=rst[b][:, 1:2],
                        scalar2=-1.0, op0=OP.mult, op1=OP.mult))
                    E("vector", vdone, inc="dve")
                    ssem = f"sxn{b}"
                    wl2 = [("dve", C("dve"))]
                    if ci - 2 in p1a_st:
                        wl2.append(p1a_st[ci - 2])
                    E("scalar", lambda e, b=b: e.activation(
                        out=xnt1[b][:], in_=xt[b][:], func=AF.Identity,
                        scale=rst[b][:, 1:2], bias=rst[b][:, 0:1]), waits=wl2, inc="act")
                    p1a_x_act[ci] = C("act")
                    E("scalar", lambda e, t0=t0, b=b: e.dma_start(
                        out=xn_d[t0:t0 + 128, :], in_=xnt1[b][:]),
                        waits=[("act", p1a_x_act[ci])], inc=ssem, inc_n=16, dma=True)
                    p1a_st[ci] = (ssem, C(ssem))
                    if ci < 20:
                        emit_wconv(2 * ci)
                        emit_wconv(2 * ci + 1)
                    if ci == 20:
                        emit_keys()
                    if ci == 21:
                        emit_wqft()
            w_done = C("pool")
            wkv_done = wstg_free.get(15, w_done)
            s2.close()

            def xn_store_waits(sc):
                got = {}
                for ci in range(min(sc * 4 + 4, NCH)):
                    if ci in p1a_st:
                        sname, v = p1a_st[ci]
                        got[sname] = max(got.get(sname, 0), v)
                return list(got.items())

            if "p1b" in PHASES:
                rec.scope = "p1b"
                k_evac_act = {0: 0}
                v_evac_act = {0: 0}
                E("tensor", lambda e: e.matmul(
                    out=bank(6), lhsT=onesr[:, 0:128], rhs=zrow[:],
                    start=True, stop=False, skip_group_check=True),
                    waits=[("dve", consts_done)])
                p1b_xnf_free = {}
                for sc in range(NSC):
                    b = sc % 2
                    t0 = sc * 512
                    xsem = f"xnf{b}"
                    wl = xn_store_waits(sc)
                    if sc - 2 in p1b_xnf_free:
                        wl.append(("pe", p1b_xnf_free[sc - 2]))
                    for ic in range(8):
                        E("sync", lambda e, t0=t0, ic=ic, b=b: e.dma_start_transpose(
                            out=xnf[b][:, ic * 512:(ic + 1) * 512],
                            in_=xn_d[t0:t0 + 512, ic * 128:(ic + 1) * 128]),
                            waits=(wl if ic == 0 else []), inc=xsem, inc_n=16, dma=True)
                    xnf_ld = C(xsem)
                    for cl in range(4):
                        ci = sc * 4 + cl
                        for oc in range(2):
                            for ic in range(8):
                                w0 = []
                                if oc == 0 and ic == 0:
                                    w0 = [(xsem, xnf_ld), ("act", k_evac_act[0]),
                                          ("dve", C("dve"))]
                                    if ci == 0:
                                        w0.append(("pool", wkv_done))
                                E("tensor", lambda e, b=b, cl=cl, ic=ic, oc=oc: e.matmul(
                                    out=bank(oc),
                                    lhsT=xnf[b][:, ic * 512 + cl * 128:ic * 512 + cl * 128 + 128],
                                    rhs=wks[:, ic * 1024 + oc * 512:ic * 1024 + (oc + 1) * 512],
                                    start=(ic == 0), stop=False), waits=w0)
                            E("tensor", lambda e, oc=oc: e.matmul(
                                out=bank(oc), lhsT=onesr[:, 0:128],
                                rhs=brow[:, D + oc * 512:D + (oc + 1) * 512],
                                start=False, stop=True), inc=("pe" if oc == 1 else None))
                        kpe = C("pe")
                        for pc in range(2):
                            E("scalar", lambda e, pc=pc: e.activation(
                                out=esb[:], in_=pall[:, pc * 512:(pc + 1) * 512],
                                func=AF.Exp),
                                waits=[("dve", C("dve"))] + ([("pe", kpe)]
                                                             if pc == 0 else []))
                            E("scalar", lambda e, pc=pc: e.activation(
                                out=rsb[:], in_=pall[:, pc * 512:(pc + 1) * 512],
                                func=AF.Relu), inc="act")
                            kact = C("act")
                            E("vector", lambda e, pc=pc: e.scalar_tensor_tensor(
                                out=kpsb[:, pc * 512:(pc + 1) * 512], in0=esb[:],
                                scalar=1.0,
                                in1=rsb[:], op0=OP.min, op1=OP.add),
                                waits=[("act", kact)])
                            E("vector", vdone, inc="dve")
                        k_evac_act[0] = kact
                        kdve = C("dve")
                        for oc in range(2):
                            for ic in range(8):
                                w0 = [("act", v_evac_act[0])] if (oc == 0 and ic == 0) else []
                                E("tensor", lambda e, b=b, cl=cl, ic=ic, oc=oc: e.matmul(
                                    out=bank(2 + oc),
                                    lhsT=xnf[b][:, ic * 512 + cl * 128:ic * 512 + cl * 128 + 128],
                                    rhs=wvs[:, ic * 1024 + oc * 512:ic * 1024 + (oc + 1) * 512],
                                    start=(ic == 0), stop=False), waits=w0)
                            E("tensor", lambda e, oc=oc: e.matmul(
                                out=bank(2 + oc), lhsT=onesr[:, 0:128],
                                rhs=brow[:, 2 * D + oc * 512:2 * D + (oc + 1) * 512],
                                start=False, stop=True), inc=("pe" if oc == 1 else None))
                        vpe = C("pe")
                        E("scalar", lambda e: e.activation(out=vpsb[:], in_=bank(2, 2),
                                                           func=AF.Copy),
                          waits=[("pe", vpe), ("dve", C("dve"))], inc="act")
                        vact = C("act")
                        v_evac_act[0] = vact
                        for h in range(16):
                            par, col = h % 2, h // 2
                            E("tensor", lambda e, h=h, ci=ci, par=par, col=col: e.matmul(
                                out=pall[par * 64:par * 64 + 64,
                                         6 * 512 + col * 64:6 * 512 + (col + 1) * 64],
                                lhsT=kpsb[:, h * 64:(h + 1) * 64],
                                rhs=vpsb[:, h * 64:(h + 1) * 64],
                                start=False, stop=(ci == NCH - 1),
                                skip_group_check=True),
                                waits=([("dve", kdve), ("act", vact)] if h == 0 else []),
                                inc=("pe" if h == 15 else None))
                    p1b_xnf_free[sc] = C("pe")
                kv_done = C("pe")
                E("vector", lambda e: e.tensor_copy(out=kvsb[:], in_=bank(6)),
                  waits=[("pe", kv_done)])
                E("vector", vdone, inc="dve")
            kv_evac = C("dve")
            s1.close()


            p2_ss_free = {}

            def p2_piece_units(sc, piece, st):
                b = sc % 2
                t0 = sc * 512
                units = []
                if piece == 0:
                    def mk_q_mm(wv, oc4):
                        def u():
                            oc = wv * 4 + oc4
                            for ic in range(8):
                                w0 = []
                                if oc4 == 0 and ic == 0:
                                    if wv == 0:
                                        w0 = [(st["xsem"], st["xnf_ld"]),
                                              ("act", C("act")), ("dve", C("dve"))]
                                    else:
                                        w0 = [("act", st["q_relu0"])]
                                E("tensor", lambda e, b=b, ic=ic, oc=oc, oc4=oc4: e.matmul(
                                    out=bank(oc4),
                                    lhsT=wqs[:, ic * 1024 + oc * 128:ic * 1024 + oc * 128 + 128],
                                    rhs=xnf[b][:, ic * 512:(ic + 1) * 512],
                                    start=(ic == 0), stop=False), waits=w0)
                            E("tensor", lambda e, oc=oc, oc4=oc4: e.matmul(
                                out=bank(oc4), lhsT=brow[:, oc * 128:(oc + 1) * 128],
                                rhs=onesr[:, 0:512], start=False, stop=True),
                                inc=("pe" if oc4 == 3 else None))
                        return u

                    def mk_q_elu(wv):
                        def u():
                            qpe = C("pe")
                            for pc in range(4):
                                E("scalar", lambda e, pc=pc: e.activation(
                                    out=esb[:], in_=pall[:, pc * 512:(pc + 1) * 512],
                                    func=AF.Exp),
                                    waits=[("dve", C("dve"))] + ([("pe", qpe)]
                                                                 if pc == 0 else []))
                                E("scalar", lambda e, pc=pc: e.activation(
                                    out=rsb[:], in_=pall[:, pc * 512:(pc + 1) * 512],
                                    func=AF.Relu), inc="act")
                                r_act = C("act")
                                E("vector", lambda e, wv=wv, pc=pc: e.scalar_tensor_tensor(
                                    out=qps[:, wv * 2048 + pc * 512:wv * 2048 + (pc + 1) * 512],
                                    in0=esb[:], scalar=1.0,
                                    in1=rsb[:], op0=OP.min, op1=OP.add),
                                    waits=[("act", r_act)])
                                E("vector", vdone, inc="dve")
                                st["q_relu0"] = r_act
                            st["q_dve"] = C("dve")
                        return u

                    for wv in range(2):
                        for oc4 in range(4):
                            units.append(mk_q_mm(wv, oc4))
                        units.append(mk_q_elu(wv))
                elif piece == 1:
                    def mk_g_mm(wv, oc4):
                        def u():
                            oc = wv * 4 + oc4
                            for ic in range(8):
                                w0 = []
                                if oc4 == 0 and ic == 0:
                                    w0 = ([("dve", st["q_dve"])] if wv == 0
                                          else [("act", st["g_sig0"])])
                                E("tensor", lambda e, b=b, ic=ic, oc=oc, oc4=oc4: e.matmul(
                                    out=bank(oc4),
                                    lhsT=wgs[:, ic * 1024 + oc * 128:ic * 1024 + oc * 128 + 128],
                                    rhs=xnf[b][:, ic * 512:(ic + 1) * 512],
                                    start=(ic == 0), stop=False), waits=w0)
                            E("tensor", lambda e, oc=oc, oc4=oc4: e.matmul(
                                out=bank(oc4),
                                lhsT=brow[:, 3 * D + oc * 128:3 * D + oc * 128 + 128],
                                rhs=onesr[:, 0:512], start=False, stop=True),
                                inc=("pe" if oc4 == 3 else None))
                        return u

                    def mk_g_sig(wv):
                        def u():
                            gpe = C("pe")
                            E("scalar", lambda e, wv=wv: e.activation(
                                out=gts[:, wv * 2048:(wv + 1) * 2048], in_=pall[:, 0:2048],
                                func=AF.Sigmoid), waits=[("pe", gpe)], inc="act")
                            st["g_sig0"] = C("act")
                            if wv == 1:
                                st["g_sig1"] = C("act")
                        return u

                    for wv in range(2):
                        for oc4 in range(4):
                            units.append(mk_g_mm(wv, oc4))
                        units.append(mk_g_sig(wv))
                elif piece == 2:
                    def mk_attn(wv, hh4):
                        def u():
                            hh = wv * 4 + hh4
                            for par in range(2):
                                h = 2 * hh + par
                                w0 = []
                                if hh4 == 0 and par == 0:
                                    w0 = ([("act", st["g_sig1"])] if wv == 0
                                          else [("dve", C("dve"))])
                                E("tensor", lambda e, hh4=hh4, par=par, h=h: e.matmul(
                                    out=pall[par * 64:par * 64 + 64,
                                             hh4 * 512:(hh4 + 1) * 512],
                                    lhsT=kvsb[par * 64:par * 64 + 64,
                                              (h // 2) * 64:(h // 2 + 1) * 64],
                                    rhs=qps[par * 64:par * 64 + 64,
                                            (h // 2) * 512:(h // 2 + 1) * 512],
                                    start=True, stop=(par == 1), skip_group_check=True),
                                    waits=w0, inc=("pe" if par == 1 else None))
                            ape = C("pe")
                            E("vector", lambda e, hh4=hh4, hh=hh: e.tensor_tensor(
                                out=gas[:, hh * 512:(hh + 1) * 512], in0=bank(hh4),
                                in1=gts[:, hh * 512:(hh + 1) * 512], op=OP.mult),
                                waits=[("pe", ape)])
                            if hh4 == 3:
                                E("vector", vdone, inc="dve")
                                if wv == 1:
                                    st["ga_dve"] = C("dve")
                        return u

                    for wv in range(2):
                        for hh4 in range(4):
                            units.append(mk_attn(wv, hh4))
                elif piece == 3:
                    def mk_wo(tsub, oc):
                        def u():
                            sb = ssb[tsub % 2]
                            ssem = f"ss{tsub % 2}"
                            for ic in range(8):
                                w0 = []
                                if oc == 0 and ic == 0:
                                    w0 = [("dve", st["ga_dve"]), ("act", C("act"))]
                                E("tensor", lambda e, ic=ic, oc=oc, tsub=tsub: e.matmul(
                                    out=bank(oc),
                                    lhsT=gas[:, ic * 512 + tsub * 128:ic * 512 + tsub * 128 + 128],
                                    rhs=wos[:, ic * 1024 + oc * 512:ic * 1024 + (oc + 1) * 512],
                                    start=(ic == 0), stop=False), waits=w0)
                            E("tensor", lambda e, oc=oc: e.matmul(
                                out=bank(oc), lhsT=onesr[:, 0:128],
                                rhs=brow[:, 4 * D + oc * 512:4 * D + (oc + 1) * 512],
                                start=False, stop=True), inc=("pe" if oc == 1 else None))
                            if oc == 1:
                                spe = C("pe")
                                wse = [("pe", spe)]
                                if ssem in p2_ss_free:
                                    wse.append((ssem, p2_ss_free[ssem]))
                                E("scalar", lambda e, sb=sb: e.activation(
                                    out=sb[:, 0:512], in_=bank(0), func=AF.Copy),
                                    waits=wse)
                                E("scalar", lambda e, sb=sb: e.activation(
                                    out=sb[:, 512:1024], in_=bank(1), func=AF.Copy),
                                    inc="act")
                                E("scalar", lambda e, t0=t0, tsub=tsub, sb=sb: e.dma_start(
                                    out=s_d[t0 + tsub * 128:t0 + tsub * 128 + 128, :],
                                    in_=sb[:]),
                                    waits=[("act", C("act"))], inc=ssem, inc_n=16,
                                    dma=True)
                                p2_ss_free[ssem] = C(ssem)
                        return u

                    for tsub in range(4):
                        for oc in range(2):
                            units.append(mk_wo(tsub, oc))
                return units

            if "peer" in PHASES:
                rec.scope = "peer"
                with ExitStack() as c2:
                    xnt2 = [c2.enter_context(nc.sbuf_tensor(f"xnt2_{i}", [128, 4096], BF16))
                            for i in range(2)]
                    qfs = [c2.enter_context(nc.sbuf_tensor(f"qfs{i}", [64, 8 * 512], BF16))
                           for i in range(2)]
                    NT = 12
                    gt = [c2.enter_context(nc.sbuf_tensor(f"gt{i}", [128, 2048], BF16))
                          for i in range(NT)]
                    sxy = c2.enter_context(nc.sbuf_tensor("sxy", [128, 64], F32))
                    ixy = c2.enter_context(nc.sbuf_tensor("ixy", [128, 64], U32))
                    asf = c2.enter_context(nc.sbuf_tensor("asf", [128, 256], F32))
                    aif = c2.enter_context(nc.sbuf_tensor("aif", [128, 256], F32))
                    ixyf = asf[:, 0:64]
                    sc8 = c2.enter_context(nc.sbuf_tensor("sc8", [128, 32], F32))
                    posu = c2.enter_context(nc.sbuf_tensor("posu", [128, 32], U32))
                    posf = c2.enter_context(nc.sbuf_tensor("posf", [128, 32], F32))
                    fidxf = c2.enter_context(nc.sbuf_tensor("fidxf", [128, 32], F32))
                    fidxu2 = c2.enter_context(nc.sbuf_tensor("fidxu2", [128, 64], U32))
                    negm4 = c2.enter_context(nc.sbuf_tensor("negm4", [128, 4], F32))
                    esc2 = c2.enter_context(nc.sbuf_tensor("esc2", [128, 64], F32))
                    sw32 = c2.enter_context(nc.sbuf_tensor("sw32", [128, 32], F32))
                    ssum42 = c2.enter_context(nc.sbuf_tensor("ssum42", [128, 16], F32))
                    wv32 = c2.enter_context(nc.sbuf_tensor("wv32", [128, 32], F32))
                    ccs = c2.enter_context(nc.sbuf_tensor("ccs", [128, 32], F32))
                    diag8 = c2.enter_context(nc.sbuf_tensor("diag8_0", [128, 8 * 128], BF16))
                    junkb = c2.enter_context(nc.sbuf_tensor("junkb", [128, 1024], BF16))
                    expt = c2.enter_context(nc.sbuf_tensor("expt", [128, 1024], BF16))

                    gt_free = {}
                    exp_free = 0
                    first_vec = [True]
                    fidx_dve_c = {}
                    mmc = {}

                    def qf_wave_units(sc, wv, st):
                        b = sc % 2
                        units = []

                        def mk_mm(ph4):
                            def u():
                                ph = wv * 4 + ph4
                                ph_off = (ph // 4) * 256 + (ph % 4) * 64
                                for ic in range(8):
                                    w0 = []
                                    if ph4 == 0 and ic == 0:
                                        w0 = [(st["xsem"], st["xnf_ld"]),
                                              ("act", C("act")), ("dve", C("dve"))]
                                    E("tensor", lambda e, b=b, ic=ic, ph=ph, ph4=ph4,
                                        ph_off=ph_off: e.matmul(
                                        out=pall[0:64, (4 + ph4) * 512:(4 + ph4 + 1) * 512],
                                        lhsT=wqfT[:, ic * 512 + ph_off:ic * 512 + ph_off + 64],
                                        rhs=xnf[b][:, ic * 512:(ic + 1) * 512],
                                        start=(ic == 0), stop=(ic == 7),
                                        skip_group_check=True), waits=w0,
                                        inc=("pe" if (ic == 7) else None))
                            return u

                        for ph4 in range(4):
                            units.append(mk_mm(ph4))

                        def evac():
                            qf_pe = C("pe")
                            for ph4 in range(4):
                                ph = wv * 4 + ph4
                                E("scalar", lambda e, ph=ph, ph4=ph4, b=b: e.activation(
                                    out=qfs[b][0:64, ph * 512:(ph + 1) * 512],
                                    in_=pall[0:64, (4 + ph4) * 512:(4 + ph4 + 1) * 512],
                                    func=AF.Copy),
                                    waits=([("pe", qf_pe)] if ph4 == 0 else []),
                                    inc="act")
                        units.append(evac)
                        return units

                    def emit_qf_wave(sc, wv, st):
                        for u in qf_wave_units(sc, wv, st):
                            u()

                    def emit_schead_a(sc):
                        b = sc % 2
                        t0 = sc * 512
                        xsem = f"xnf{b}"
                        tsem = f"xnt{b}"
                        for ic in range(8):
                            E("sync", lambda e, t0=t0, ic=ic, b=b: e.dma_start_transpose(
                                out=xnf[b][:, ic * 512:(ic + 1) * 512],
                                in_=xn_d[t0:t0 + 512, ic * 128:(ic + 1) * 128]),
                                waits=([("pe", C("pe")), ("dve", C("dve"))]
                                       if ic == 0 else []),
                                inc=xsem, inc_n=16, dma=True)
                        xnf_ld = C(xsem)
                        E("sync", lambda e, t0=t0, b=b: e.dma_start(
                            out=xnt2[b][:].rearrange("p (c d) -> p c d", d=1024),
                            in_=xn_d[t0:t0 + 512, :].rearrange("(c p) d -> p c d", p=128)),
                            waits=[("dve", C("dve"))], inc=tsem, inc_n=16, dma=True)
                        xnt_ld = C(tsem)
                        return dict(xsem=xsem, xnf_ld=xnf_ld, tsem=tsem, xnt_ld=xnt_ld)

                    def emit_sim(ci, st):
                        sc, cl = ci // 4, ci % 4
                        b = sc % 2
                        pb0 = (4 + 2 * (ci % 2)) * 512
                        for ph in range(8):
                            w0 = []
                            if ph == 0:
                                w0 = [("act", C("act")), ("dve", C("dve"))]
                            E("tensor", lambda e, ph=ph, cl=cl, b=b, pb0=pb0: e.matmul(
                                out=pall[:, pb0 + ph * 128:pb0 + (ph + 1) * 128],
                                lhsT=qfs[b][0:64, ph * 512 + cl * 128:ph * 512 + cl * 128 + 128],
                                rhs=keysT[0:64, ph * 128:(ph + 1) * 128],
                                start=True, stop=True, skip_group_check=True), waits=w0,
                                inc=("pe" if ph == 7 else None))
                        return C("pe")

                    def emit_topk(ci, st, sim_pe):
                        sc, cl = ci // 4, ci % 4
                        b = sc % 2
                        par = ci % 2
                        pb0 = (4 + 2 * (ci % 2)) * 512
                        for ph in range(8):
                            w0 = []
                            if ph == 0:
                                w0 = [("pe", sim_pe)]
                                if first_vec[0]:
                                    w0.append(("pool", iota_done))
                                    first_vec[0] = False
                            E("vector", lambda e, ph=ph, pb0=pb0: e.max(
                                out=sxy[:, ph * 8:(ph + 1) * 8],
                                in_=pall[:, pb0 + ph * 128:pb0 + (ph + 1) * 128]),
                                waits=w0)
                            E("vector", lambda e: e.drain())
                            E("vector", lambda e, ph=ph, pb0=pb0: e.max_index(
                                out=ixy[:, ph * 8:(ph + 1) * 8],
                                in_max=sxy[:, ph * 8:(ph + 1) * 8],
                                in_values=pall[:, pb0 + ph * 128:pb0 + (ph + 1) * 128]))
                        E("vector", vdone, inc="dve")
                        E("vector", lambda e: e.tensor_copy(out=ixyf[:], in_=ixy[:]))
                        E("vector", lambda e: e.drain())
                        E("vector", lambda e: e.tensor_scalar(
                            out=posf[:], in0=ixyf[:, 0:32], scalar1=128.0, scalar2=None,
                            op0=OP.mult))
                        E("vector", lambda e: e.drain())
                        E("vector", lambda e: e.tensor_tensor(
                            out=aif[:].rearrange("p (h a b) -> p h a b", h=4, a=8),
                            in0=posf[:].rearrange("p (h a) -> p h a", h=4).unsqueeze(3)
                                .broadcast_to([128, 4, 8, 8]),
                            in1=ixyf[:, 32:64].rearrange("p (h b) -> p h b", h=4).unsqueeze(2)
                                .broadcast_to([128, 4, 8, 8]), op=OP.add))
                        E("vector", lambda e: e.tensor_tensor(
                            out=asf[:, 64:256].rearrange("p (x a b) -> p x a b", x=3, a=8),
                            in0=sxy[:, 8:32].rearrange("p (x a) -> p x a", x=3).unsqueeze(3)
                                .broadcast_to([128, 3, 8, 8]),
                            in1=sxy[:, 40:64].rearrange("p (x b) -> p x b", x=3).unsqueeze(2)
                                .broadcast_to([128, 3, 8, 8]), op=OP.add))
                        E("vector", lambda e: e.tensor_tensor(
                            out=asf[:, 0:64].rearrange("p (a b) -> p a b", a=8),
                            in0=sxy[:, 0:8].unsqueeze(2).broadcast_to([128, 8, 8]),
                            in1=sxy[:, 32:40].unsqueeze(1).broadcast_to([128, 8, 8]),
                            op=OP.add))
                        E("vector", lambda e: e.drain())
                        for h in range(4):
                            E("vector", lambda e, h=h: e.max(
                                out=sc8[:, h * 8:(h + 1) * 8], in_=asf[:, h * 64:(h + 1) * 64]))
                            E("vector", lambda e: e.drain())
                            E("vector", lambda e, h=h: e.max_index(
                                out=posu[:, h * 8:(h + 1) * 8],
                                in_max=sc8[:, h * 8:(h + 1) * 8],
                                in_values=asf[:, h * 64:(h + 1) * 64]))
                        E("vector", lambda e: e.drain())
                        E("vector", lambda e: e.tensor_copy(out=posf[:], in_=posu[:]))
                        E("vector", lambda e: e.drain())
                        for hs in range(32):
                            h = hs // 8
                            E("vector", lambda e, h=h, hs=hs: e.scalar_tensor_tensor(
                                out=sxy[:, 0:64], in0=iota64[:], scalar=posf[:, hs:hs + 1],
                                in1=aif[:, h * 64:(h + 1) * 64], op0=OP.is_equal,
                                op1=OP.mult, accum_out=fidxf[:, hs:hs + 1]))
                        E("vector", lambda e: e.drain())
                        E("vector", lambda e, par=par: e.tensor_copy(
                            out=fidxu2[:, par * 32:(par + 1) * 32], in_=fidxf[:]))
                        E("vector", vdone, inc="dve")
                        fidx_dve_c[ci] = C("dve")
                        if DBGP and ci == 0:
                            E("scalar", lambda e: e.dma_start(
                                out=dbgP[0, :, 0:32], in_=fidxf[:]),
                                waits=[("dve", fidx_dve_c[ci])], inc="dp", inc_n=16,
                                dma=True)
                        E("vector", lambda e: e.tensor_scalar(
                            out=negm4[:].unsqueeze(2),
                            in0=sc8[:].rearrange("p (h a) -> p h a", a=8)[:, :, 0:1],
                            scalar1=-1.0, scalar2=None, op0=OP.mult))
                        E("vector", vdone, inc="dve")
                        neg_dve = C("dve")
                        for h in range(4):
                            E("scalar", lambda e, h=h, par=par: e.activation(
                                out=esc2[:, par * 32 + h * 8:par * 32 + (h + 1) * 8],
                                in_=sc8[:, h * 8:(h + 1) * 8],
                                func=AF.Exp, bias=negm4[:, h:h + 1]),
                                waits=([("dve", neg_dve)] if h == 0 else []), inc="act")
                        esc_act = C("act")
                        E("vector", lambda e, par=par: e.tensor_reduce(
                            out=ssum42[:, par * 8:par * 8 + 4],
                            in_=esc2[:, par * 32:(par + 1) * 32]
                                .rearrange("p (h a) -> p h a", h=4),
                            axis=mybir.AxisListType.X, op=OP.add), waits=[("act", esc_act)])
                        E("vector", lambda e: e.drain())
                        E("vector", lambda e, par=par: e.reciprocal(
                            out=ssum42[:, par * 8 + 4:par * 8 + 8],
                            in_=ssum42[:, par * 8:par * 8 + 4]))
                        E("vector", lambda e: e.drain())
                        E("vector", lambda e, par=par: e.tensor_tensor(
                            out=esc2[:, par * 32:(par + 1) * 32]
                                .rearrange("p (h a) -> p h a", h=4),
                            in0=esc2[:, par * 32:(par + 1) * 32]
                                .rearrange("p (h a) -> p h a", h=4),
                            in1=ssum42[:, par * 8 + 4:par * 8 + 8].unsqueeze(2)
                                .broadcast_to([128, 4, 8]), op=OP.mult))
                        E("vector", vdone, inc="dve")
                        if DBGP and ci == 0:
                            E("scalar", lambda e, par=par: e.dma_start(
                                out=dbgP[1, :, 0:64], in_=esc2[:]),
                                waits=[("dve", C("dve"))], inc="dp", inc_n=16,
                                dma=True)
                            E("scalar", lambda e: e.dma_start(
                                out=dbgP[2, :, 0:16], in_=ssum42[:]),
                                waits=[("dve", C("dve"))], inc="dp", inc_n=16,
                                dma=True)

                    def emit_gather_phase(ci, st, fillers):
                        sc, cl = ci // 4, ci % 4
                        b = sc % 2
                        par = ci % 2
                        t0c = ci * 128
                        eb = 4 + 2 * par
                        gbase = ci * 8
                        fi = [0]

                        def pop_filler():
                            if fi[0] < len(fillers):
                                fillers[fi[0]]()
                                fi[0] += 1
                        for oc in range(2):
                            E("tensor", lambda e, oc=oc, eb=eb: e.matmul(
                                out=bank(eb + oc), lhsT=onesr[:, 0:128], rhs=zrow[:],
                                start=True, stop=False, skip_group_check=True),
                                waits=([("act", C("act")), ("dve", C("dve"))]
                                       if oc == 0 else []))
                        ttc = {}
                        sic = {}
                        ccc = {}
                        dgc = {}
                        for phs in range(11):
                            pop_filler()
                            if 3 <= phs <= 10:
                                q = phs - 3
                                for sl in range(4):
                                    hs = q * 4 + sl
                                    tl = (ci * 32 + hs) % NT
                                    for oc in range(2):
                                        w0 = ([("dve", dgc[q])]
                                              if (sl == 0 and oc == 0) else [])
                                        E("tensor", lambda e, sl=sl, tl=tl, oc=oc,
                                            hs=hs, eb=eb, q=q: e.matmul(
                                            out=bank(eb + oc),
                                            lhsT=diag8[:, (q % 2) * 512 + sl * 128:
                                                       (q % 2) * 512 + (sl + 1) * 128],
                                            rhs=gt[tl][:, 1024 + oc * 512:1024 + (oc + 1) * 512],
                                            start=False, stop=(hs == 31),
                                            skip_group_check=True), waits=w0,
                                            inc=("pe" if oc == 1 else None))
                                    gt_free[tl] = C("pe")
                                mmc[gbase + q] = C("pe")
                            if 2 <= phs <= 9:
                                q = phs - 2
                                E("scalar", lambda e, q=q: e.activation(
                                    out=sw32[:, q * 4:(q + 1) * 4],
                                    in_=wv32[:, q * 4:(q + 1) * 4],
                                    func=AF.Silu),
                                    waits=([] if DOTMODE == "actaccum"
                                           else [("dve", ttc[q])]), inc="act")
                                sic[q] = C("act")
                                E("vector", lambda e, q=q, par=par: e.tensor_tensor(
                                    out=ccs[:, q * 4:(q + 1) * 4],
                                    in0=sw32[:, q * 4:(q + 1) * 4],
                                    in1=esc2[:, par * 32 + q * 4:par * 32 + (q + 1) * 4],
                                    op=OP.mult), waits=[("act", sic[q])])
                                E("vector", lambda e: e.drain())
                                wdg = []
                                if gbase + q - 2 in mmc:
                                    wdg = [("pe", mmc[gbase + q - 2])]
                                for sl in range(4):
                                    hs = q * 4 + sl
                                    E("vector", lambda e, sl=sl, hs=hs, q=q: e.tensor_scalar(
                                        out=diag8[:, (q % 2) * 512 + sl * 128:
                                                   (q % 2) * 512 + (sl + 1) * 128],
                                        in0=eyebf[:], scalar1=ccs[:, hs:hs + 1],
                                        scalar2=None, op0=OP.mult),
                                        waits=(wdg if sl == 0 else []))
                                E("vector", vdone, inc="dve")
                                dgc[q] = C("dve")
                            if 1 <= phs <= 8 and DOTMODE == "actaccum":
                                q = phs - 1
                                for sl in range(4):
                                    hs = q * 4 + sl
                                    tl = (ci * 32 + hs) % NT
                                    E("scalar", lambda e, tl=tl, hs=hs: e.activation(
                                        out=gt[tl][:, 0:1024], in_=gt[tl][:, 0:1024],
                                        func=AF.Copy, accum_out=wv32[:, hs:hs + 1]),
                                        waits=([("dve", ttc[q])] if sl == 0 else []))
                            if phs < 8:
                                q = phs
                                for sl in range(4):
                                    hs = q * 4 + sl
                                    tl = (ci * 32 + hs) % NT
                                    gsem = f"g{tl}"
                                    wl = [("dve", fidx_dve_c[ci])]
                                    if tl in gt_free:
                                        wl.append(("pe", gt_free[tl]))
                                    E("gpsimd", lambda e, hs=hs, tl=tl, par=par:
                                        e.indirect_dma_start(
                                            out=gt[tl][:], out_offset=None, in_=tbl2,
                                            in_offset=bass.IndirectOffsetOnAxis(
                                                ap=fidxu2[:, par * 32 + hs:par * 32 + hs + 1],
                                                axis=0)),
                                        waits=wl, inc=gsem, inc_n=16, dma=True)
                                    gld = C(gsem)
                                    wt = [(gsem, gld)]
                                    if hs == 0:
                                        wt.append((st["tsem"], st["xnt_ld"]))
                                    if DOTMODE == "actaccum":
                                        E("vector", lambda e, tl=tl, b=b, cl=cl:
                                            e.tensor_tensor(
                                            out=gt[tl][:, 0:1024], in0=gt[tl][:, 0:1024],
                                            in1=xnt2[b][:, cl * 1024:(cl + 1) * 1024],
                                            op=OP.mult), waits=wt)
                                        if sl == 3:
                                            E("vector", vdone, inc="dve")
                                    else:
                                        E("vector", lambda e, tl=tl, b=b, cl=cl, hs=hs:
                                            e.scalar_tensor_tensor(
                                            out=junkb[:], in0=gt[tl][:, 0:1024],
                                            scalar=1.0,
                                            in1=xnt2[b][:, cl * 1024:(cl + 1) * 1024],
                                            op0=OP.mult, op1=OP.mult,
                                            accum_out=wv32[:, hs:hs + 1]), waits=wt)
                                        if sl == 3:
                                            E("vector", vdone, inc="dve")
                                ttc[q] = C("dve")
                        while fi[0] < len(fillers):
                            pop_filler()
                        if DBGP and ci == 0:
                            E("scalar", lambda e: e.dma_start(
                                out=dbgP[3, :, 0:32], in_=wv32[:]),
                                waits=[("dve", ttc[7])], inc="dp", inc_n=16, dma=True)
                            E("scalar", lambda e: e.dma_start(
                                out=dbgP[4, :, 0:32], in_=ccs[:]),
                                waits=[("dve", dgc[7])], inc="dp", inc_n=16, dma=True)
                            E("scalar", lambda e: e.dma_start(
                                out=dbgP[5, :, 0:32], in_=sw32[:]),
                                waits=[("dve", dgc[7])], inc="dp", inc_n=16, dma=True)
                        E("vector", lambda e, eb=eb: e.tensor_copy(out=expt[:], in_=bank(eb, 2)),
                          waits=[("pe", C("pe")), ("se", exp_free)])
                        E("vector", vdone, inc="dve")
                        E("scalar", lambda e, t0c=t0c: e.dma_start(
                            out=exp_d[t0c:t0c + 128, :], in_=expt[:]),
                            waits=[("dve", C("dve"))], inc="se", inc_n=16, dma=True)
                        return C("se")

                    st_by_sc = {0: emit_schead_a(0)}
                    emit_qf_wave(0, 0, st_by_sc[0])
                    emit_qf_wave(0, 1, st_by_sc[0])
                    for ci in range(-1, NCH):
                        nci = ci + 1
                        fillers = []
                        if nci < NCH:
                            nsc = nci // 4
                            if nci % 4 == 1 and nsc + 1 < NSC:
                                st_by_sc[nsc + 1] = emit_schead_a(nsc + 1)
                                emit_qf_wave(nsc + 1, 0, st_by_sc[nsc + 1])
                            if nci % 4 == 2 and nsc + 1 < NSC:
                                emit_qf_wave(nsc + 1, 1, st_by_sc[nsc + 1])
                            fillers += p2_piece_units(nsc, nci % 4, st_by_sc[nsc])
                            sim_pe = emit_sim(nci, st_by_sc[nsc])
                            emit_topk(nci, st_by_sc[nsc], sim_pe)
                        if ci >= 0:
                            exp_free = emit_gather_phase(ci, st_by_sc[ci // 4], fillers)
                        else:
                            for u in fillers:
                                u()

        E("sync", lambda e: e.nop(), waits=rec.all_dma_waits(), inc="bar")
        E("scalar", lambda e: e.nop(), waits=[("act", C("act"))], inc="bar")
        E("vector", vdone, waits=[("dve", C("dve"))], inc="bar")
        E("tensor", lambda e: e.nop(), waits=[("pe", C("pe"))], inc="bar")
        E("gpsimd", lambda e: e.nop(), waits=[("pool", C("pool"))], inc="bar")
        bar2 = C("bar")

        if "p3" in PHASES:
            rec.scope = "p3"
            with ExitStack() as c3:
                wgT = c3.enter_context(nc.sbuf_tensor("wgT", [128, 16384], BF16))
                wuT = c3.enter_context(nc.sbuf_tensor("wuT", [128, 16384], BF16))
                wdT = c3.enter_context(nc.sbuf_tensor("wdT", [128, 16384], BF16))
                xnf3 = [c3.enter_context(nc.sbuf_tensor(f"xnf3_{i}", [128, 4096], BF16))
                        for i in range(2)]
                sg = [c3.enter_context(nc.sbuf_tensor(f"sg{i}", [128, 512], BF16))
                      for i in range(2)]
                inter2 = [c3.enter_context(nc.sbuf_tensor(f"inter{i}", [128, 8192], BF16))
                          for i in range(2)]
                m1t = [c3.enter_context(nc.sbuf_tensor(f"m1t{i}", [128, 1024], BF16))
                       for i in range(2)]
                sldt = [c3.enter_context(nc.sbuf_tensor(f"sldt{i}", [128, 1024], BF16))
                        for i in range(2)]
                xldt = [c3.enter_context(nc.sbuf_tensor(f"xldt{i}", [128, 1024], F32))
                        for i in range(2)]
                eldt = [c3.enter_context(nc.sbuf_tensor(f"eldt{i}", [128, 1024], BF16))
                        for i in range(2)]
                cmbt = [c3.enter_context(nc.sbuf_tensor(f"cmb{i}", [128, 1024], F32))
                        for i in range(2)]
                outt = [c3.enter_context(nc.sbuf_tensor(f"outt{i}", [128, 1024], F32))
                        for i in range(2)]

                m1_free = {}
                out_free = {}
                mlp_bank_free = {}
                icc_pe = {}
                half0_stores = []
                for half in range(2):
                    rec.scope = f"p3_{half}"
                    h0 = half * 2048
                    for ic in range(8):
                        E("sync", lambda e, ic=ic, h0=h0: e.dma_start_transpose(
                            out=wgT[:, ic * 2048:(ic + 1) * 2048],
                            in_=wgbf[h0:h0 + 2048, ic * 128:(ic + 1) * 128]),
                            waits=([("bar", bar2), ("pe", C("pe"))] if ic == 0 else []),
                            inc="wl3", inc_n=16, dma=True)
                        E("sync", lambda e, ic=ic, h0=h0: e.dma_start_transpose(
                            out=wuT[:, ic * 2048:(ic + 1) * 2048],
                            in_=wubf[h0:h0 + 2048, ic * 128:(ic + 1) * 128]),
                            inc="wl3", inc_n=16, dma=True)
                    wld = C("wl3")

                    wd_pe_prev = {}

                    def ld_xnf3(sc, pe_w=None):
                        b = sc % 2
                        t0 = sc * 512
                        xsem = f"xnf{b}"
                        for ic in range(8):
                            E("sync", lambda e, t0=t0, ic=ic, b=b: e.dma_start_transpose(
                                out=xnf3[b][:, ic * 512:(ic + 1) * 512],
                                in_=xn_d[t0:t0 + 512, ic * 128:(ic + 1) * 128]),
                                waits=([("pe", pe_w if pe_w is not None else C("pe"))]
                                       if ic == 0 else []),
                                inc=xsem, inc_n=16, dma=True)
                        return C(xsem)

                    xn_ld_next = ld_xnf3(0)
                    for it in range(16):
                        E("sync", lambda e, it=it, h0=h0: e.dma_start_transpose(
                            out=wdT[:, it * 1024:(it + 1) * 1024],
                            in_=wdbf[:, h0 + it * 128:h0 + it * 128 + 128]),
                            inc="wld3", inc_n=16, dma=True)
                    wld_d = C("wld3")
                    for sc in range(NSC):
                        b = sc % 2
                        t0 = sc * 512
                        xsem = f"xnf{b}"
                        itb = inter2[sc % 2]
                        xn_ld = xn_ld_next
                        silu_cnt = {}
                        tt_cnt = {}
                        for icc in range(16):
                            bg = (icc % 2) * 2
                            bu = bg + 1
                            for ic in range(8):
                                w0 = []
                                if ic == 0:
                                    w0 = [(xsem, xn_ld), ("wl3", wld)]
                                    if icc - 2 in silu_cnt:
                                        w0 += [("act", silu_cnt[icc - 2]),
                                               ("dve", tt_cnt[icc - 2])]
                                    elif icc < 2 and sc == 0 and half == 0:
                                        w0 += [("dve", C("dve")), ("act", C("act"))]
                                    elif icc < 2:
                                        w0 += [("act", C("act")), ("dve", C("dve"))]
                                    if icc == 0:
                                        w0.append(("pe", C("pe")))
                                E("tensor", lambda e, b=b, ic=ic, icc=icc, bg=bg: e.matmul(
                                    out=bank(bg),
                                    lhsT=wgT[:, ic * 2048 + icc * 128:ic * 2048 + icc * 128 + 128],
                                    rhs=xnf3[b][:, ic * 512:(ic + 1) * 512],
                                    start=(ic == 0), stop=(ic == 7)), waits=w0)
                            for ic in range(8):
                                E("tensor", lambda e, b=b, ic=ic, icc=icc, bu=bu: e.matmul(
                                    out=bank(bu),
                                    lhsT=wuT[:, ic * 2048 + icc * 128:ic * 2048 + icc * 128 + 128],
                                    rhs=xnf3[b][:, ic * 512:(ic + 1) * 512],
                                    start=(ic == 0), stop=(ic == 7)),
                                    inc=("pe" if ic == 7 else None))
                            iu_pe = C("pe")
                            sgt = sg[icc % 2]
                            E("scalar", lambda e, bg=bg, sgt=sgt: e.activation(
                                out=sgt[:], in_=bank(bg), func=AF.Silu),
                                waits=[("pe", iu_pe), ("dve", C("dve"))], inc="act")
                            silu_cnt[icc] = C("act")
                            ttw = [("act", C("act"))]
                            if icc == 0 and (sc % 2) in wd_pe_prev:
                                ttw.append(("pe", wd_pe_prev[sc % 2]))
                            E("vector", lambda e, bu=bu, icc=icc, sgt=sgt, itb=itb:
                                e.tensor_tensor(
                                    out=itb[:, icc * 512:(icc + 1) * 512], in0=bank(bu),
                                    in1=sgt[:], op=OP.mult), waits=ttw)
                            E("vector", vdone, inc="dve")
                            tt_cnt[icc] = C("dve")
                        int_dve = C("dve")
                        icc_pe[sc % 2] = iu_pe
                        if sc + 1 < NSC:
                            xn_ld_next = ld_xnf3(sc + 1, icc_pe.get((sc + 1) % 2))
                        for tsub in range(4):
                            t0c = t0 + tsub * 128
                            bb = tsub % 2
                            csem = f"cmb{bb}"
                            if half == 1:
                                wl = [("dve", C("dve"))] + half0_stores
                                E("sync", lambda e, t0c=t0c, bb=bb: e.dma_start(
                                    out=m1t[bb][:], in_=mlp1_d[t0c:t0c + 128, :]),
                                    waits=wl, inc=csem, inc_n=16, dma=True)
                                E("sync", lambda e, t0c=t0c, bb=bb: e.dma_start(
                                    out=sldt[bb][:], in_=s_d[t0c:t0c + 128, :]),
                                    inc=csem, inc_n=16, dma=True)
                                E("sync", lambda e, t0c=t0c, bb=bb: e.dma_start(
                                    out=xldt[bb][:], in_=x_d[t0c:t0c + 128, :]),
                                    inc=csem, inc_n=16, dma=True)
                                E("sync", lambda e, t0c=t0c, bb=bb: e.dma_start(
                                    out=eldt[bb][:], in_=exp_d[t0c:t0c + 128, :]),
                                    inc=csem, inc_n=16, dma=True)
                            cmb_ld = C(csem)
                            mb0 = 4 + 2 * (tsub % 2) if half == 1 else 4
                            for oc in range(2):
                                for icc in range(16):
                                    w0 = []
                                    if oc == 0 and icc == 0:
                                        w0 = [("dve", int_dve), ("act", C("act")),
                                              ("wld3", wld_d)]
                                        if half == 1 and (tsub % 2) in mlp_bank_free:
                                            w0.append(("dve", mlp_bank_free[tsub % 2]))
                                    E("tensor", lambda e, icc=icc, oc=oc, tsub=tsub,
                                        itb=itb, mb0=mb0: e.matmul(
                                        out=bank(mb0 + oc),
                                        lhsT=itb[:, icc * 512 + tsub * 128:icc * 512 + tsub * 128 + 128],
                                        rhs=wdT[:, icc * 1024 + oc * 512:icc * 1024 + (oc + 1) * 512],
                                        start=(icc == 0), stop=(icc == 15)), waits=w0,
                                        inc=("pe" if (icc == 15 and oc == 1) else None))
                            mlp_pe = C("pe")
                            if half == 0:
                                msem = f"sm{bb}"
                                wse = [("pe", mlp_pe)]
                                if msem in m1_free:
                                    wse.append((msem, m1_free[msem]))
                                E("scalar", lambda e, bb=bb: e.activation(
                                    out=m1t[bb][:, 0:512], in_=bank(4), func=AF.Copy),
                                    waits=wse)
                                E("scalar", lambda e, bb=bb: e.activation(
                                    out=m1t[bb][:, 512:1024], in_=bank(5), func=AF.Copy),
                                    inc="act")
                                E("scalar", lambda e, t0c=t0c, bb=bb: e.dma_start(
                                    out=mlp1_d[t0c:t0c + 128, :], in_=m1t[bb][:]),
                                    waits=[("act", C("act"))], inc=msem, inc_n=16, dma=True)
                                m1_free[msem] = C(msem)
                                E("tensor", lambda e: e.nop(), waits=[("act", C("act"))],
                                  inc="pe")
                            else:
                                cb = cmbt[bb]
                                ot = outt[bb]
                                E("vector", lambda e, cb=cb, bb=bb, mb0=mb0: e.tensor_tensor(
                                    out=cb[:, 0:512], in0=bank(mb0), in1=m1t[bb][:, 0:512],
                                    op=OP.add), waits=[("pe", mlp_pe), (csem, cmb_ld)])
                                E("vector", lambda e, cb=cb, bb=bb, mb0=mb0: e.tensor_tensor(
                                    out=cb[:, 512:1024], in0=bank(mb0 + 1),
                                    in1=m1t[bb][:, 512:1024], op=OP.add))
                                E("vector", lambda e: e.drain())
                                E("vector", lambda e, cb=cb, bb=bb: e.tensor_tensor(
                                    out=cb[:], in0=cb[:], in1=eldt[bb][:], op=OP.add))
                                E("vector", vdone, inc="dve")
                                mlp_bank_free[tsub % 2] = C("dve")
                                E("scalar", lambda e, cb=cb: e.activation(
                                    out=cb[:], in_=cb[:], func=AF.Silu),
                                    waits=[("dve", C("dve"))], inc="act")
                                E("vector", lambda e, cb=cb, bb=bb: e.tensor_tensor(
                                    out=cb[:], in0=cb[:], in1=sldt[bb][:], op=OP.mult),
                                    waits=[("act", C("act"))])
                                E("vector", lambda e: e.drain())
                                osem = f"so{bb}"
                                wl = [] if osem not in out_free else [(osem, out_free[osem])]
                                E("vector", lambda e, cb=cb, ot=ot, bb=bb: e.tensor_tensor(
                                    out=ot[:], in0=cb[:], in1=xldt[bb][:], op=OP.add),
                                    waits=wl)
                                E("vector", vdone, inc="dve")
                                E("scalar", lambda e, t0c=t0c, ot=ot: e.dma_start(
                                    out=out_d[t0c:t0c + 128, :], in_=ot[:]),
                                    waits=[("dve", C("dve"))], inc=osem, inc_n=16, dma=True)
                                out_free[osem] = C(osem)
                                E("tensor", lambda e: e.nop(), waits=[("dve", C("dve"))],
                                  inc="pe")
                        wd_pe_prev[sc % 2] = C("pe")
                    if half == 0:
                        half0_stores = [(s_, rec.c(s_)) for s_ in ("sm0", "sm1")
                                        if rec.c(s_) > 0]

        if DEBUG and dbg_d is not None:
            srcs = [xn_d, s_d, exp_d, mlp1_d][:n_dbg]
            for i, sd in enumerate(srcs):
                E("gpsimd", lambda e, i=i, sd=sd: e.dma_start(out=dbg_d[i], in_=sd),
                  waits=rec.all_dma_waits(), inc="dbgc", inc_n=16, dma=True)

        E("sync", lambda e: e.nop(), waits=rec.all_dma_waits(), inc="bar")

        names = set(rec.cnt)
        for o in rec.ops:
            for (sname, v) in o["waits"]:
                names.add(sname)
        sems = {n: en(nc.semaphore(n)) for n in sorted(names)}
        block = en(nc.Block())

        def runner(name):
            def _run(e):
                cur = None
                cm = None
                for o in rec.ops:
                    if o["eng"] != name:
                        continue
                    if o.get("scope") != cur:
                        if cm is not None:
                            cm.__exit__(None, None, None)
                        cur = o.get("scope")
                        cm = nc.named_scope(cur)
                        cm.__enter__()
                    for (sname, v) in o["waits"]:
                        e.wait_ge(sems[sname], v)
                    ins = o["fn"](e)
                    if o["inc"] is not None:
                        sname, n = o["inc"]
                        ins.then_inc(sems[sname], n)
                if cm is not None:
                    cm.__exit__(None, None, None)
            return _run

        block.sync(runner("sync"))
        block.gpsimd(runner("gpsimd"))
        block.vector(runner("vector"))
        block.scalar(runner("scalar"))
        block.tensor(runner("tensor"))

    return nc


def make_in_maps(inputs, n_cores=N_CORES):
    x = np.ascontiguousarray(inputs["L0_attn_x"])
    bias5 = np.ascontiguousarray(np.stack([
        inputs["L0_attn_bq"], inputs["L0_attn_bk"], inputs["L0_attn_bv"],
        inputs["L0_attn_bg"], inputs["L0_attn_bo"]], 0))
    base = dict(
        wq=np.ascontiguousarray(inputs["L0_attn_Wq"]),
        wk=np.ascontiguousarray(inputs["L0_attn_Wk"]),
        wv=np.ascontiguousarray(inputs["L0_attn_Wv"]),
        wg=np.ascontiguousarray(inputs["L0_attn_Wg"]),
        wo=np.ascontiguousarray(inputs["L0_attn_Wo"]),
        bias5=bias5,
        wqf=np.ascontiguousarray(inputs["L0_ffn_Wq"]),
        keys=np.ascontiguousarray(inputs["L0_ffn_keys"]),
        down=np.ascontiguousarray(inputs["L0_ffn_down_embed"]),
        up=np.ascontiguousarray(inputs["L0_ffn_up_embed"]),
        wgf=np.ascontiguousarray(inputs["L0_ffn_Wg"]),
        wuf=np.ascontiguousarray(inputs["L0_ffn_Wu"]),
        wdf=np.ascontiguousarray(inputs["L0_ffn_Wd"]),
        eye=np.eye(128, dtype=np.float32),
    )
    return [dict(base, x=np.ascontiguousarray(x[c])) for c in range(n_cores)]


def kernel(**inputs):
    in_maps = make_in_maps(inputs)
    nc = build()
    res = run_bass_kernel_spmd(nc, in_maps, core_ids=list(range(N_CORES)))
    return np.stack([r["out"] for r in res.results], 0)
